# version 19
# baseline (speedup 1.0000x reference)
from collections import defaultdict
from contextlib import ExitStack

import numpy as np
import ml_dtypes
import concourse.bass as bass
import concourse.mybir as mybir
from concourse.alu_op_type import AluOpType as ALU
from concourse.bass_utils import run_bass_kernel_spmd

F32 = mybir.dt.float32
BF16 = mybir.dt.bfloat16
I32 = mybir.dt.int32
AF = mybir.ActivationFunctionType
AX = mybir.AxisListType

D, DK, DV, NH, HK, HV = 2048, 1024, 2048, 32, 32, 64
RLO, GLO, FH, EPS = 32, 64, 5632, 1e-5
BATCH = 8
ATTN_SUB = 99
ATTN_NCH = None
GN_SUB = 99
C = 128
NB = 32
ACLAMP = 40.0


class Prog:
    ENGINES = ("pe", "act", "dve", "pool", "sp")

    def __init__(self, nc, dma_streams=()):
        self.nc = nc
        self.dma_streams = list(dma_streams)
        self.ops = []
        self.cnt = {e: 0 for e in self.ENGINES}
        self.dma_cnt = {s: 0 for s in self.dma_streams}
        self.clock = {e: defaultdict(int) for e in self.ENGINES}
        self.last_w = {}
        self.readers = defaultdict(dict)
        self.waited = defaultdict(set)

    def _dep_waits(self, eng, reads, writes):
        waits = []

        def need(sem_key, val):
            if sem_key in self.dma_cnt:
                val = 16 * self.dma_cnt[sem_key]
            if self.clock[eng][sem_key] >= val:
                return
            self.clock[eng][sem_key] = val
            for i, (sk, _v) in enumerate(waits):
                if sk == sem_key:
                    waits[i] = (sem_key, val)
                    return
            waits.append((sem_key, val))

        for b in reads:
            w = self.last_w.get(b)
            if w is not None:
                need(*w)
        for b in writes:
            w = self.last_w.get(b)
            if w is not None:
                need(*w)
            for sk, v in self.readers.get(b, {}).items():
                need(sk, v)
        return waits

    def op(self, eng, emit, reads=(), writes=()):
        waits = self._dep_waits(eng, reads, writes)
        self.cnt[eng] += 1
        val = self.cnt[eng]
        self.ops.append((eng, waits, emit, eng, val))
        for b in reads:
            self.readers[b][eng] = val
        for b in writes:
            self.last_w[b] = (eng, val)
            self.readers[b] = {}
        for sk, v in waits:
            self.waited[sk].add(v)
        return val

    def dma(self, queue, stream, emit, reads=(), writes=()):
        waits = self._dep_waits(queue, reads, writes)
        self.dma_cnt[stream] += 1
        val = 16 * self.dma_cnt[stream]
        self.ops.append((queue, waits, emit, stream, val))
        for b in reads:
            self.readers[b][stream] = val
        for b in writes:
            self.last_w[b] = (stream, val)
            self.readers[b] = {}
        for sk, v in waits:
            self.waited[sk].add(v)
        return val

    def barrier(self):
        snap_eng = {e: self.cnt[e] for e in self.ENGINES}
        snap_dma = dict(self.dma_cnt)
        for e in self.ENGINES:
            waits = []
            for e2, v in snap_eng.items():
                if e2 != e and v and self.clock[e][e2] < v:
                    self.clock[e][e2] = v
                    waits.append((e2, v))
                    self.waited[e2].add(v)
            for s, n in snap_dma.items():
                if n and self.clock[e][s] < 16 * n:
                    self.clock[e][s] = 16 * n
                    waits.append((s, 16 * n))
            if waits:
                self.ops.append((e, waits, None, None, 0))

    def barrier_wait(self, eng, sem_key, val):
        if self.clock[eng][sem_key] >= val:
            return
        self.clock[eng][sem_key] = val
        self.ops.append((eng, [(sem_key, val)], None, None, 0))
        self.waited[sem_key].add(val)

    def dma_total(self, stream):
        return 16 * self.dma_cnt[stream]

    def build(self):
        nc = self.nc
        remap, inc_needed = {}, {}
        for e in self.ENGINES:
            need = set(self.waited.get(e, ()))
            inc_needed[e] = need
            remap[e] = {v: i + 1 for i, v in enumerate(sorted(need))}

        def mapped(sk, v):
            return remap[sk][v] if sk in remap else v

        with ExitStack() as stack:
            sems = {}
            for e in self.ENGINES:
                if self.cnt[e] or self.waited.get(e):
                    sems[e] = stack.enter_context(nc.semaphore("sem_" + e))
            for s in self.dma_streams:
                if self.dma_cnt[s]:
                    sems[s] = stack.enter_context(nc.semaphore("dma_" + s))
            block = stack.enter_context(nc.Block())
            per = {e: [] for e in self.ENGINES}
            for rec in self.ops:
                per[rec[0]].append(rec)

            def run(engine_obj, eng_name):
                for (_e, waits, emit, inc_key, val) in per[eng_name]:
                    for sk, v in waits:
                        engine_obj.wait_ge(sems[sk], mapped(sk, v))
                    if emit is None:
                        continue
                    inst = emit(engine_obj)
                    if inc_key in remap:
                        if val in inc_needed[inc_key]:
                            inst.then_inc(sems[inc_key], 1)
                    else:
                        inst.then_inc(sems[inc_key], 16)

            if per["sp"]:
                @block.sync
                def _(e):
                    run(e, "sp")
            if per["pe"]:
                @block.tensor
                def _(e):
                    run(e, "pe")
            if per["act"]:
                @block.scalar
                def _(e):
                    run(e, "act")
            if per["dve"]:
                @block.vector
                def _(e):
                    run(e, "dve")
            if per["pool"]:
                @block.gpsimd
                def _(e):
                    run(e, "pool")


class Arena:

    def __init__(self, tensor, cols):
        self.t = tensor
        self.cols = cols
        self.off = 0

    def reset(self):
        self.off = 0

    def alloc(self, ncols_bf16):
        ncols_bf16 = (ncols_bf16 + 63) & ~63
        o = self.off
        self.off += ncols_bf16
        assert self.off <= self.cols, f"arena overflow: {self.off} > {self.cols}"
        return o

    def bf(self, ncols):
        o = self.alloc(ncols)
        return self.t[:, o:o + ncols]

    def f32(self, ncols):
        o = self.alloc(2 * ncols)
        return self.t[:, o:o + 2 * ncols].bitcast(F32)


def bcast(sl, inner):
    pat = [list(p) for p in sl.ap] + [[0, inner]]
    return bass.AP(sl.tensor, sl.offset, pat)


def host_constants():
    c = {}
    j = np.arange(128)[:, None]
    i = np.arange(128)[None, :]
    c["ntri_excl"] = -(j < i).astype(np.float32)
    c["ntri_incl"] = -(j <= i).astype(np.float32)
    c["mask2"] = (((j // NB) == (i // NB)) & (j < i)).astype(ml_dtypes.bfloat16)
    c["ident"] = np.eye(128).astype(ml_dtypes.bfloat16)
    c["zero32"] = np.zeros((32, 128), dtype=ml_dtypes.bfloat16)
    c["onesf"] = np.ones((1, 128), dtype=np.float32)
    bd = np.zeros((128, 256), dtype=ml_dtypes.bfloat16)
    for h4 in range(4):
        bd[32 * h4:32 * h4 + 32, 64 * h4:64 * h4 + 64] = 1.0
    c["bdmask"] = bd
    sel = np.zeros((128, 4), dtype=ml_dtypes.bfloat16)
    for h4 in range(4):
        sel[32 * h4:32 * h4 + 32, h4] = 1.0
    c["sel4"] = sel
    return c


CONST_SHAPES = {k: (list(v.shape), F32 if v.dtype == np.float32 else BF16)
                for k, v in host_constants().items()}


def build_program(T=2048, stage="full", dbg=()):
    NCH = T // C
    TB = min(512, T)
    NTB = T // TB
    FB = min(512, T)
    NFB = T // FB
    nc = bass.Bass("TRN2", target_bir_lowering=False)

    def din(name, shape, dt=F32):
        return nc.dram_tensor(name, shape, dt, kind="ExternalInput").ap()

    xTok = din("x", [T, D])
    xFt = din("xT", [D, T])
    Wx1 = din("Wx1", [D, 5 * RLO])
    Wx2 = din("Wx2", [5 * RLO, D])
    Wr = din("Wr", [D, DK])
    Ww1 = din("Ww1", [D, GLO])
    Ww2 = din("Ww2", [GLO, DK])
    bw = din("bw", [1, DK])
    Wk = din("Wk", [D, DK])
    Wv = din("Wv", [D, DV])
    Wg = din("Wg", [D, DV])
    Wo = din("Wo", [DV, D])
    wq_gT = din("wq_gT", [FH // 128, 128, D], I32)
    wq_uT = din("wq_uT", [FH // 128, 128, D], I32)
    wq_dT = din("wq_dT", [FH // 128, 128, D], I32)

    s_d = din("s_d", [1, D])
    vt_mu = din("vt_mu", [128, 16])
    vt_bonus = din("vt_bonus", [128, 8])
    vt_gnw = din("vt_gnw", [128, 16])
    vt_gnb = din("vt_gnb", [128, 16])
    vt_sg = din("vt_sg", [128, 44])
    vt_su = din("vt_su", [128, 44])
    CH = {k: din("c_" + k, s, dt) for k, (s, dt) in CONST_SHAPES.items()}
    out = nc.dram_tensor("out", [T, D], F32, kind="ExternalOutput").ap()

    dbg_out = {}

    def mkdbg(name, shape, dt=F32):
        if name in dbg:
            dbg_out[name] = nc.dram_tensor("dbg_" + name, shape, dt,
                                           kind="ExternalOutput").ap()
        return dbg_out.get(name)

    def scr(name, shape, dt=BF16):
        if ("S:" + name) in dbg:
            t = nc.dram_tensor(name, shape, dt, kind="ExternalOutput")
            dbg_out["S:" + name] = name
            return t.ap()
        return nc.dram_tensor(name, shape, dt).ap()

    WCOLS = {"wx1": 5 * RLO, "wr": DK, "ww1": GLO, "wk": DK, "wv": DV, "wg": DV, "wo": D}
    wc = {n: scr(n + "_c", [D, c]) for n, c in WCOLS.items()}
    r_s = scr("r_s", [NCH, 128, 8 * C])
    k_s = scr("k_s", [NCH, 128, 8 * C])
    sg_s = scr("sg_s", [NCH, 128, 16 * C])
    v_s = scr("v_s", [T, DV])
    x1_s = scr("x1_s", [T, D])
    fg_s = scr("fg_s", [FH // 128, 128, 2048])
    fu_s = scr("fu_s", [FH // 128, 128, 2048])
    fd_s = scr("fd_s", [FH // 128, 128, 2048])

    P = Prog(nc, dma_streams=["sc", "ldf0", "ldf1", "wc0", "s0", "s1", "s2", "s3",
                              "cwx1", "cwr", "cww1", "cwk", "cwv", "cwg", "cwo",
                              "c0", "c1", "xr", "xo", "f0", "f1", "fw0",
                              "fw1", "fw2", "outq", "r0", "r1", "k0", "k1", "v0", "v1", "wc1",
                              "g0", "g1", "x1q", "fg0", "fg1", "fg2", "fu0", "fu1", "fu2", "fd0", "fd1", "fd2", "wd0", "wd1"])
    stack = ExitStack()
    SB = lambda name, shape, dt=BF16: stack.enter_context(nc.sbuf_tensor(name, shape, dt))
    pb = [stack.enter_context(nc.psum_tensor(f"pb{i}", [128, 512], F32)) for i in range(8)]
    PBK = [f"pb{i}" for i in range(8)]

    consts = {k: SB("k_" + k, s, dt) for k, (s, dt) in CONST_SHAPES.items()}
    vt = {"mu": SB("vt_mu_s", [128, 16], F32), "bonus": SB("vt_bonus_s", [128, 8], F32),
          "gnw": SB("vt_gnw_s", [128, 16], F32), "gnb": SB("vt_gnb_s", [128, 16], F32),
          "sg": SB("vt_sg_s", [128, 44], F32), "su": SB("vt_su_s", [128, 44], F32)}
    sbw = SB("sbw", [1, DK], F32)
    sWw2 = SB("sWw2", [GLO, DK], BF16)
    swx2a = SB("swx2a", [128, D], BF16)
    swx2b = SB("swx2b", [32, D], BF16)
    tanhu = SB("tanhu", [GLO, T], BF16)
    zf = SB("zf", [128, 2 * T], BF16)
    Sst = SB("Sst", [128, 8 * 256], F32)
    Sbf = SB("Sbf", [128, 8 * 256], BF16)
    sdb = SB("sdb", [128, D], F32)
    csc = SB("csc", [128, 2], F32)
    ARENA_COLS = 68352
    AR = Arena(SB("arena", [128, ARENA_COLS], BF16), ARENA_COLS)

    for k in consts:
        P.dma("sp", "sc", lambda E, k=k: E.dma_start(consts[k][:, :], CH[k]), writes=["k_" + k])
    for nm, src in [("mu", vt_mu), ("bonus", vt_bonus), ("gnw", vt_gnw),
                    ("gnb", vt_gnb), ("sg", vt_sg), ("su", vt_su)]:
        P.dma("sp", "sc", lambda E, nm=nm, src=src: E.dma_start(vt[nm][:, :], src), writes=["vt_" + nm])
    P.dma("sp", "sc", lambda E: E.dma_start(sbw[:, :], bw), writes=["sbw"])

    ldf = [SB(f"ldf{i}", [128, 2048], F32) for i in range(2)]
    ldb = [SB(f"ldb{i}", [128, 2048], BF16) for i in range(2)]
    jobs = []
    for name, src in [("wx1", Wx1), ("wr", Wr), ("ww1", Ww1), ("wk", Wk),
                      ("wv", Wv), ("wg", Wg), ("wo", Wo)]:
        cols = WCOLS[name]
        for dt in range(16):
            jobs.append((src[dt * 128:(dt + 1) * 128, :], wc[name][dt * 128:(dt + 1) * 128, :], 128, cols, name))
    for idx, (sap, dap, rows, cols, nmx) in enumerate(jobs):
        sl = idx % 2
        P.dma("sp", f"ldf{sl}", lambda E, sap=sap, rows=rows, cols=cols, sl=sl:
              E.dma_start(ldf[sl][0:rows, 0:cols], sap), writes=[f"ldf{sl}"])
        P.op("act", lambda E, rows=rows, cols=cols, sl=sl:
             E.copy(ldb[sl][0:rows, 0:cols], ldf[sl][0:rows, 0:cols]),
             reads=[f"ldf{sl}"], writes=[f"ldb{sl}"])
        P.dma("act", "c" + nmx, lambda E, dap=dap, rows=rows, cols=cols, sl=sl:
              E.dma_start(dap, ldb[sl][0:rows, 0:cols]), reads=[f"ldb{sl}"], writes=[("wc", nmx)])
    P.dma("sp", "ldf0", lambda E: E.dma_start(ldf[0][0:128, 0:D], Wx2[0:128, :]), writes=["ldf0"])
    P.op("act", lambda E: E.copy(swx2a[:, :], ldf[0][0:128, 0:D]), reads=["ldf0"], writes=["swx2a"])
    P.dma("sp", "ldf1", lambda E: E.dma_start(ldf[1][0:32, 0:D], Wx2[128:160, :]), writes=["ldf1"])
    P.op("act", lambda E: E.copy(swx2b[:, :], ldf[1][0:32, 0:D]), reads=["ldf1"], writes=["swx2b"])
    P.dma("sp", "ldf0", lambda E: E.dma_start(ldf[0][0:GLO, 0:DK], Ww2), writes=["ldf0"])
    P.op("act", lambda E: E.copy(sWw2[:, :], ldf[0][0:GLO, 0:DK]), reads=["ldf0"], writes=["sWw2"])
    P.dma("sp", "ldf1", lambda E: E.dma_start(ldf[1][0:1, 0:D], s_d), writes=["ldf1"])
    for q in range(4):
        P.op("pe", lambda E, q=q: E.matmul(pb[q][:, 0:512], consts["onesf"][0:1, :],
                                           ldf[1][0:1, 512 * q:512 * (q + 1)],
                                           start=True, stop=True),
             reads=["ldf1", "k_onesf"], writes=[PBK[q]])
        P.op("dve", lambda E, q=q: E.tensor_copy(sdb[:, 512 * q:512 * (q + 1)], pb[q][:, 0:512]),
             reads=[PBK[q]], writes=["sdb"])
    P.op("dve", lambda E: E.memset(csc[:, 0:1], EPS), writes=["csc"])
    P.op("dve", lambda E: E.memset(csc[:, 1:2], -0.5), writes=["csc"])
    P.op("dve", lambda E: E.memset(zf[:, :], 0.0), writes=["zf"])
    P.op("dve", lambda E: E.memset(Sst[:, :], 0.0), writes=["Sst"])
    P.op("dve", lambda E: E.memset(Sbf[:, :], 0.0), writes=["Sbf"])

    xTb = AR.bf(16 * T)
    xiT2 = [AR.bf(16 * TB), AR.bf(16 * TB)]
    wcur = AR.bf(16 * 1024)
    evb = [AR.bf(512) for _ in range(4)]

    for dt in range(16):
        sl = dt % 2
        P.dma("sp", f"ldf{sl}", lambda E, dt=dt, sl=sl:
              E.dma_start(ldf[sl][:, 0:T], xFt[dt * 128:(dt + 1) * 128, 0:T]), writes=[f"ldf{sl}"])
        P.op("act", lambda E, dt=dt, sl=sl: E.copy(xTb[:, T * dt:T * (dt + 1)], ldf[sl][:, 0:T]),
             reads=[f"ldf{sl}"], writes=["xTb"])

    JOBS = [("z", "wx1", "z"), ("r", "wr", "feat"), ("w", "ww1", "u"),
            ("k", "wk", "feat"), ("v", "wv", "tok"), ("g", "wg", "feat")]
    JCOLS = {"z": 160, "r": DK, "w": GLO, "k": DK, "v": DV, "g": DV}
    JDST = {"r": r_s, "k": k_s, "g": sg_s}

    def load_half(name, c0, cols, hf):
        hkey = f"wcur{hf}"
        off = 512 * hf
        ctot = WCOLS[name]
        for g, queue in ((0, "sp"), (1, "act")):
            strm = f"wc{hf}" if g == 0 else f"wd{hf}"
            P.dma(queue, strm, lambda E, name=name, g=g, c0=c0, cols=cols, off=off, ctot=ctot:
                  E.dma_start(bass.AP(wcur.tensor, wcur.offset + 1024 * 8 * g + off,
                                      [list(wcur.ap[0]), [1024, 8], [1, cols]]),
                              bass.AP(wc[name].tensor, (8 * g * 128) * ctot + c0,
                                      [[ctot, 128], [128 * ctot, 8], [1, cols]])),
                  reads=[("wc", name)], writes=[hkey])

    def xi_build(tb, j, slot):
        t0 = tb * TB
        xiT = xiT2[slot]
        xkey = f"xiT{slot}"
        if j > 0:
            fi = j - 1
            lhs, lkey, zrow = (swx2a, "swx2a", 32 * fi) if fi < 4 else (swx2b, "swx2b", 0)
        xstr = lambda off: bass.AP(xTb.tensor, xTb.offset + off,
                                   [list(p) for p in xTb.ap[:1]] + [[T, 16], [1, TB]])
        xiv = xiT[:, :].rearrange("p (a b) -> p a b", b=TB)
        if t0 == 0:
            for dt in range(16):
                P.op("dve", lambda E, dt=dt, xiT=xiT:
                     E.tensor_scalar(xiT[:, TB * dt:TB * dt + 1], xTb[:, T * dt:T * dt + 1],
                                     -1.0, None, ALU.mult), reads=["xTb", xkey], writes=[xkey])
                P.op("dve", lambda E, dt=dt, xiT=xiT:
                     E.tensor_tensor(xiT[:, TB * dt + 1:TB * dt + TB],
                                     xTb[:, T * dt:T * dt + TB - 1],
                                     xTb[:, T * dt + 1:T * dt + TB], ALU.subtract),
                     reads=["xTb", xkey], writes=[xkey])
        else:
            P.op("dve", lambda E, t0=t0, xiv=xiv, xstr=xstr:
                 E.tensor_tensor(xiv, xstr(t0 - 1), xstr(t0), ALU.subtract),
                 reads=["xTb", xkey], writes=[xkey])
        if j == 0:
            P.op("dve", lambda E, xiv=xiv:
                 E.tensor_tensor(xiv, xiv, bcast(vt["mu"][:, 0:16], TB), ALU.mult),
                 reads=[xkey, "vt_mu"], writes=[xkey])
        else:
            for dt in range(16):
                bank = 6 + (dt % 2)
                def mmu(E, dt=dt, bank=bank, zrow=zrow, lhs=lhs, t0=t0, fi=fi):
                    zsrc = zf[zrow:zrow + 32, t0:t0 + TB] if fi < 4 else zf[0:32, T + t0:T + t0 + TB]
                    return E.matmul(pb[bank][:, 0:TB],
                                    lhs[zrow:zrow + 32, dt * 128:(dt + 1) * 128] if fi < 4
                                    else lhs[0:32, dt * 128:(dt + 1) * 128],
                                    zsrc, start=True, stop=True, tile_position=(zrow, 0))
                P.op("pe", mmu, reads=[lkey, "zf"], writes=[PBK[bank]])
                P.op("dve", lambda E, dt=dt, bank=bank, xiT=xiT:
                     E.tensor_tensor(xiT[:, TB * dt:TB * dt + TB], xiT[:, TB * dt:TB * dt + TB],
                                     pb[bank][:, 0:TB], ALU.mult),
                     reads=[xkey, PBK[bank]], writes=[xkey])
        P.op("dve", lambda E, t0=t0, xiv=xiv, xstr=xstr:
             E.tensor_tensor(xiv, xiv, xstr(t0), ALU.add),
             reads=[xkey, "xTb"], writes=[xkey])

    def job_mms(tb, j, slot):
        name, wname, kind = JOBS[j]
        t0 = tb * TB
        xiT = xiT2[slot]
        xkey = f"xiT{slot}"
        cols_total = JCOLS[name]
        nbatch = (cols_total + 511) // 512
        out = []
        for bidx in range(nbatch):
            c0 = 512 * bidx
            cols = min(512, cols_total - c0)
            hf = bidx % 2
            hkey = f"wcur{hf}"
            off = 512 * hf
            load_half(wname, c0, cols, hf)
            batch_ops = []
            if kind == "tok":
                for tt in range(TB // 128):
                    bank = 4 + (tt % 2)
                    def mm(E, tt=tt, bank=bank, off=off, xiT=xiT):
                        for kt in range(16):
                            m = E.matmul(pb[bank][:, 0:512],
                                         xiT[:, TB * kt + 128 * tt:TB * kt + 128 * (tt + 1)],
                                         wcur[:, 1024 * kt + off:1024 * kt + off + 512],
                                         start=(kt == 0), stop=(kt == 15))
                        return m
                    P.op("pe", mm, reads=[hkey, xkey], writes=[PBK[bank]])
                    sl = tt % 4
                    P.op("act", lambda E, bank=bank, sl=sl:
                         E.copy(evb[sl][:, 0:512], pb[bank][:, 0:512]),
                         reads=[PBK[bank]], writes=[f"evb{sl}"])
                    P.dma("act", f"s{sl}", lambda E, tt=tt, c0=c0, sl=sl, t0=t0:
                          E.dma_start(v_s[t0 + 128 * tt:t0 + 128 * (tt + 1), c0:c0 + 512],
                                      evb[sl][:, 0:512]),
                          reads=[f"evb{sl}"], writes=[("spillv", t0, tt, c0)])
            else:
                nml = (cols + 127) // 128
                for ml in range(nml):
                    mt = 4 * bidx + ml
                    rows = min(128, cols - 128 * ml)
                    bank = 4 + (ml % 2)
                    def mm(E, ml=ml, bank=bank, rows=rows, off=off, xiT=xiT):
                        for kt in range(16):
                            m = E.matmul(pb[bank][0:rows, 0:TB],
                                         wcur[:, 1024 * kt + off + 128 * ml:1024 * kt + off + 128 * ml + rows],
                                         xiT[:, TB * kt:TB * kt + TB], start=(kt == 0), stop=(kt == 15))
                        return m
                    P.op("pe", mm, reads=[hkey, xkey], writes=[PBK[bank]])
                    if kind == "z":
                        P.op("act", lambda E, mt=mt, rows=rows, bank=bank, t0=t0:
                             E.activation(zf[0:rows, T * mt + t0:T * mt + t0 + TB],
                                          pb[bank][0:rows, 0:TB], AF.Tanh),
                             reads=[PBK[bank]], writes=["zf"])
                    elif kind == "u":
                        P.op("act", lambda E, bank=bank, t0=t0:
                             E.activation(tanhu[0:GLO, t0:t0 + TB], pb[bank][0:GLO, 0:TB], AF.Tanh),
                             reads=[PBK[bank]], writes=["tanhu"])
                    else:
                        dst = JDST[name]
                        sl = mt % 4
                        if name == "g":
                            P.op("act", lambda E, sl=sl, bank=bank:
                                 E.activation(evb[sl][:, 0:TB], pb[bank][:, 0:TB], AF.Silu),
                                 reads=[PBK[bank]], writes=[f"evb{sl}"])
                        else:
                            P.op("act", lambda E, sl=sl, bank=bank:
                                 E.copy(evb[sl][:, 0:TB], pb[bank][:, 0:TB]),
                                 reads=[PBK[bank]], writes=[f"evb{sl}"])
                        for cq in range(TB // C):
                            ch = (t0 // C) + cq
                            P.dma("act", f"s{sl}", lambda E, mt=mt, sl=sl, cq=cq, ch=ch, dst=dst:
                                  E.dma_start(dst[ch, :, C * mt:C * (mt + 1)],
                                              evb[sl][:, C * cq:C * (cq + 1)]),
                                  reads=[f"evb{sl}"], writes=[("spill", id(dst), ch, mt)])
            if bidx == 0 and j + 1 < len(JOBS):
                xi_build(tb, j + 1, (j + 1) % 2)

    def build_proj():
        for tb in range(NTB):
            xi_build(tb, 0, 0)
            for j in range(len(JOBS)):
                job_mms(tb, j, j % 2)

    build_proj()

    if "zf" in dbg:
        d = mkdbg("zf", [128, T])
        P.dma("sp", "outq", lambda E, d=d: E.dma_start(d, zf[:, :].bitcast(F32)), reads=["zf"])
    if "tanhu" in dbg:
        d = mkdbg("tanhu", [GLO, T // 2])
        P.dma("sp", "outq", lambda E, d=d: E.dma_start(d, tanhu[:, :].bitcast(F32)), reads=["tanhu"])

    if stage == "proj":
        P.barrier()
        P.barrier_wait("sp", "outq", P.dma_total("outq"))
        for s in ("s0", "s1", "s2", "s3", "c0", "c1"):
            P.barrier_wait("sp", s, P.dma_total(s))
        P.op("dve", lambda E: E.memset(ldf[0][0:1, 0:4], 0.0), writes=["ldf0"])
        P.dma("sp", "outq", lambda E: E.dma_start(out[0:1, 0:4], ldf[0][0:1, 0:4].bitcast(F32)[0:1, 0:2].bitcast(F32) if False else ldf[0][0:1, 0:4]), reads=["ldf0"])
        P.barrier_wait("sp", "outq", P.dma_total("outq"))
        P.build()
        stack.close()
        return nc, dbg_out

    P.barrier()
    AR.reset()
    rF = [AR.bf(1024) for _ in range(2)]
    kF = [AR.bf(1024) for _ in range(2)]
    sgF = [AR.bf(2048) for _ in range(2)]
    vT = [AR.bf(2048) for _ in range(2)]
    expw2 = [AR.f32(1024) for _ in range(2)]
    Gx2 = [AR.f32(1024) for _ in range(2)]
    Gi2 = [AR.f32(1024) for _ in range(2)]
    tmpf2 = [AR.f32(1024) for _ in range(2)]
    eGC2 = [AR.f32(8) for _ in range(2)]
    kod2 = [[None] + [AR.bf(256 * I) for I in (1, 2, 3)] for _ in range(2)]
    rodU2 = [AR.bf(1024) for _ in range(2)]
    kd2 = [AR.bf(1024) for _ in range(2)]
    rd2 = [AR.bf(1024) for _ in range(2)]
    r0b2 = [AR.bf(1024) for _ in range(2)]
    kCf2 = [AR.bf(1024) for _ in range(2)]
    kCt2 = [AR.bf(1024) for _ in range(2)]
    tmpu2 = [AR.bf(1024) for _ in range(2)]
    uS2 = [AR.f32(32) for _ in range(2)]
    At = [AR.bf(128) for _ in range(4)]
    Bt = [AR.bf(128) for _ in range(2)]
    Du = [AR.bf(128) for _ in range(2)]
    st1 = AR.f32(32)
    st2 = AR.f32(32)
    stm = AR.f32(32)
    stv = AR.f32(32)
    str_ = AR.f32(32)
    o2b = AR.f32(2048)
    onb = AR.bf(2048)
    onT = AR.bf(2048)
    og = AR.bf(4096)
    sumb2 = [AR.f32(2048) for _ in range(2)]
    lnt = AR.f32(32)
    wob = [ldb[0][:, :], ldb[1][:, :]]
    WOBK = ["ldb0", "ldb1"]
    x1v = [ldf[0][:, :].bitcast(BF16)[:, 0:2048], ldf[0][:, :].bitcast(BF16)[:, 2048:4096]]
    X1K = ["ldf0", "ldf0"]

    def load_chunk_streams(ch):
        sl = ch % 2
        for (dst, src, nt, key, strm) in [(rF[sl], r_s, 8, f"rF{sl}", f"r{sl}"),
                                          (kF[sl], k_s, 8, f"kF{sl}", f"k{sl}"),
                                          (sgF[sl], sg_s, 16, f"sgF{sl}", f"g{sl}")]:
            P.dma("sp", strm, lambda E, dst=dst, src=src, nt=nt, ch=ch:
                  E.dma_start(dst[:, 0:128 * nt], src[ch, :, :]),
                  reads=[("spill", id(src), ch, mt) for mt in range(nt)], writes=[key])
        t0v = (ch * C // TB) * TB
        ttv = (ch * C % TB) // 128
        P.dma("sp", f"v{sl}", lambda E, ch=ch:
              E.dma_start(vT[sl][:, :], v_s[ch * C:(ch + 1) * C, :]),
              reads=[("spillv", t0v, ttv, c0) for c0 in range(0, DV, 512)],
              writes=[f"vT{sl}"])

    def attn_chunk(ch):
        sl = ch % 2
        t0 = ch * C
        if ATTN_SUB < 1:
            return
        expw, Gx, Gi, tmpf = expw2[sl], Gx2[sl], Gi2[sl], tmpf2[sl]
        eGCs, rodU, kd, rd = eGC2[sl], rodU2[sl], kd2[sl], rd2[sl]
        kod, r0b, kCf, kCt = kod2[sl], r0b2[sl], kCf2[sl], kCt2[sl]
        tmpu, uS = tmpu2[sl], uS2[sl]
        K_ = {k: f"{k}{sl}" for k in ("expw", "Gx", "Gi", "tmpf", "eGCs", "rodU",
                                      "kd", "rd", "r0b", "kCf", "kCt", "tmpu", "uS",
                                      "kod1", "kod2", "kod3")}
        for b in range(2):
            def mmw(E, b=b):
                E.matmul(pb[4 + b][:, 0:512], tanhu[0:GLO, t0:t0 + C],
                         sWw2[0:GLO, 512 * b:512 * (b + 1)], start=True, stop=False)
                return E.matmul(pb[4 + b][:, 0:512], consts["onesf"][0:1, :],
                                sbw[0:1, 512 * b:512 * (b + 1)], start=False, stop=True)
            P.op("pe", mmw, reads=["tanhu", "sWw2", "sbw", "k_onesf"], writes=[PBK[4 + b]])
        for b in range(2):
            P.op("act", lambda E, b=b: E.activation(expw[:, 512 * b:512 * (b + 1)],
                                                    pb[4 + b][:, 0:512], AF.Exp),
                 reads=[PBK[4 + b]], writes=[K_["expw"]])
        for half, (gbuf, gkey, tri) in enumerate([(Gx, K_["Gx"], "ntri_excl"),
                                                  (Gi, K_["Gi"], "ntri_incl")]):
            for j in range(8):
                bank = 4 + 2 * half + (j // 4)
                def mmg(E, j=j, bank=bank, tri=tri):
                    return E.matmul(pb[bank][:, 128 * (j % 4):128 * (j % 4) + 128],
                                    expw[:, 128 * j:128 * (j + 1)], consts[tri][:, :],
                                    start=(j % 4 == 0), stop=(j % 4 == 3),
                                    skip_group_check=(j % 4 not in (0, 3)))
                P.op("pe", mmg, reads=[K_["expw"], "k_" + tri], writes=[PBK[bank]])
        for half, (gbuf, gkey) in enumerate([(Gx, K_["Gx"]), (Gi, K_["Gi"])]):
            for bb in range(2):
                bank = 4 + 2 * half + bb
                P.op("act", lambda E, bb=bb, bank=bank, gbuf=gbuf:
                     E.copy(gbuf[:, 512 * bb:512 * (bb + 1)], pb[bank][:, 0:512]),
                     reads=[PBK[bank]], writes=[gkey])
        if ATTN_SUB < 2:
            return
        gc_cols = Gi[:, 127:1024:128]
        P.op("act", lambda E: E.activation(eGCs[:, 0:8], gc_cols, AF.Exp),
             reads=[K_["Gi"]], writes=[K_["eGCs"]])
        P.op("dve", lambda E: E.tensor_tensor(tmpf[:, :], bcast(gc_cols, 128), Gi[:, :], ALU.subtract),
             reads=[K_["Gi"], K_["tmpf"]], writes=[K_["tmpf"]])
        P.op("act", lambda E: E.activation(kCf[:, :], tmpf[:, :], AF.Exp),
             reads=[K_["tmpf"]], writes=[K_["kCf"]])
        P.op("dve", lambda E, sl=sl: E.tensor_tensor(kCf[:, :], kCf[:, :], kF[sl][:, :], ALU.mult),
             reads=[K_["kCf"], f"kF{sl}"], writes=[K_["kCf"]])
        P.op("act", lambda E: E.activation(r0b[:, :], Gx[:, :], AF.Exp),
             reads=[K_["Gx"]], writes=[K_["r0b"]])
        P.op("dve", lambda E, sl=sl: E.tensor_tensor(r0b[:, :], r0b[:, :], rF[sl][:, :], ALU.mult),
             reads=[K_["r0b"], f"rF{sl}"], writes=[K_["r0b"]])
        midc = Gx[:, 16:1024:32]
        blkc = Gx[:, 0:1024:32]
        for kind, dstbuf, dstkey in [("kd", kd, K_["kd"]), ("rd", rd, K_["rd"]),
                                     ("rod", rodU, K_["rodU"])]:
            if kind == "kd":
                P.op("dve", lambda E: E.tensor_tensor(tmpf[:, :], bcast(midc, 32), Gi[:, :], ALU.subtract),
                     reads=[K_["Gx"], K_["Gi"], K_["tmpf"]], writes=[K_["tmpf"]])
            elif kind == "rd":
                P.op("dve", lambda E: E.tensor_tensor(tmpf[:, :], Gx[:, :], bcast(midc, 32), ALU.subtract),
                     reads=[K_["Gx"], K_["tmpf"]], writes=[K_["tmpf"]])
            else:
                P.op("dve", lambda E: E.tensor_tensor(tmpf[:, :], Gx[:, :], bcast(blkc, 32), ALU.subtract),
                     reads=[K_["Gx"], K_["tmpf"]], writes=[K_["tmpf"]])
            if kind in ("kd", "rd"):
                P.op("dve", lambda E: E.tensor_scalar_min(tmpf[:, :], tmpf[:, :], ACLAMP),
                     reads=[K_["tmpf"]], writes=[K_["tmpf"]])
            P.op("act", lambda E, dstbuf=dstbuf: E.activation(dstbuf[:, :], tmpf[:, :], AF.Exp),
                 reads=[K_["tmpf"]], writes=[dstkey])
            src, skey = (kF[sl], f"kF{sl}") if kind == "kd" else (rF[sl], f"rF{sl}")
            P.op("dve", lambda E, dstbuf=dstbuf, src=src:
                 E.tensor_tensor(dstbuf[:, :], dstbuf[:, :], src[:, :], ALU.mult),
                 reads=[dstkey, skey], writes=[dstkey])
        for I in (1, 2, 3):
            w_ = 32 * I
            aI = Gx[:, w_:1024:128]
            gi_sl = bass.AP(Gi.tensor, Gi.offset, [list(p) for p in Gi.ap[:1]] + [[128, 8], [1, w_]])
            kf_sl = bass.AP(kF[sl].tensor, kF[sl].offset, [list(p) for p in kF[sl].ap[:1]] + [[128, 8], [1, w_]])
            P.op("dve", lambda E, I=I, aI=aI, gi_sl=gi_sl, w_=w_:
                 E.tensor_tensor(kod[I][:, :].rearrange("p (a b) -> p a b", b=w_),
                                 bcast(aI, w_), gi_sl, ALU.subtract),
                 reads=[K_["Gx"], K_["Gi"], K_[f"kod{I}"]], writes=[K_[f"kod{I}"]])
            P.op("act", lambda E, I=I: E.activation(kod[I][:, :], kod[I][:, :], AF.Exp),
                 reads=[K_[f"kod{I}"]], writes=[K_[f"kod{I}"]])
            P.op("dve", lambda E, I=I, kf_sl=kf_sl, w_=w_:
                 E.tensor_tensor(kod[I][:, :].rearrange("p (a b) -> p a b", b=w_),
                                 kod[I][:, :].rearrange("p (a b) -> p a b", b=w_), kf_sl, ALU.mult),
                 reads=[K_[f"kod{I}"], f"kF{sl}"], writes=[K_[f"kod{I}"]])
        P.op("dve", lambda E, sl=sl: E.tensor_tensor(tmpu[:, :], rF[sl][:, :], kF[sl][:, :], ALU.mult),
             reads=[f"rF{sl}", f"kF{sl}", K_["tmpu"]], writes=[K_["tmpu"]])
        P.op("dve", lambda E: E.tensor_tensor(tmpu[:, :], tmpu[:, :],
                                              bcast(vt["bonus"][:, 0:8], 128), ALU.mult),
             reads=[K_["tmpu"], "vt_bonus"], writes=[K_["tmpu"]])
        if ATTN_SUB < 3:
            return
        for j in range(8):
            P.op("pe", lambda E, j=j: E.matmul(pb[4][:, 4 * j:4 * j + 4], tmpu[:, 128 * j:128 * (j + 1)],
                                               consts["sel4"][:, :], start=(j == 0), stop=(j == 7)),
                 reads=[K_["tmpu"], "k_sel4", PBK[4]], writes=[PBK[4]])
        P.op("act", lambda E: E.copy(uS[:, :], pb[4][:, 0:32]), reads=[PBK[4]], writes=[K_["uS"]])
        def rec_tr(j):
            bk = 5 + (j % 2)
            bview = pb[bk][:, 0:64].bitcast(BF16)
            P.op("pe", lambda E, j=j, bview=bview: E.transpose(bview, kCf[:, 128 * j:128 * (j + 1)],
                                                               consts["ident"][:, :]),
                 reads=[K_["kCf"], "k_ident", PBK[bk]], writes=[PBK[bk]])
        def rec_trev(j):
            bk = 5 + (j % 2)
            bview = pb[bk][:, 0:64].bitcast(BF16)
            P.op("act", lambda E, j=j, bview=bview: E.copy(kCt[:, 128 * j:128 * (j + 1)], bview),
                 reads=[PBK[bk]], writes=[K_["kCt"]])
        rec_tr(0)
        for j in range(1, 8):
            rec_tr(j)
            rec_trev(j - 1)
        rec_trev(7)
        if ATTN_SUB < 4:
            return
        for j in range(8):
            ob = j // 2
            P.op("pe", lambda E, j=j, ob=ob:
                 E.matmul(pb[ob][:, 256 * (j % 2):256 * (j % 2) + 256], r0b[:, 128 * j:128 * (j + 1)],
                          Sbf[:, 256 * j:256 * (j + 1)], start=(j % 2 == 0), stop=False,
                          skip_group_check=(j % 2 == 1)),
                 reads=[K_["r0b"], "Sbf", PBK[ob]], writes=[PBK[ob]])
        for j in range(8):
            sb_ = 4 + j // 2
            P.op("pe", lambda E, j=j, sb_=sb_, sl=sl:
                 E.matmul(pb[sb_][:, 256 * (j % 2):256 * (j % 2) + 256], kCt[:, 128 * j:128 * (j + 1)],
                          vT[sl][:, 256 * j:256 * (j + 1)], start=(j % 2 == 0), stop=(j % 2 == 1)),
                 reads=[K_["kCt"], f"vT{sl}", PBK[sb_]], writes=[PBK[sb_]])
        for j in range(8):
            sb_ = 4 + j // 2
            co = 256 * (j % 2)
            so = 256 * j
            P.op("dve", lambda E, sb_=sb_, co=co:
                 E.tensor_tensor(o2b[:, 0:256], pb[sb_][:, co:co + 256], consts["bdmask"][:, :], ALU.mult),
                 reads=[PBK[sb_], "k_bdmask", "o2b"], writes=["o2b"])
            P.op("dve", lambda E, j=j, so=so:
                 E.tensor_scalar(Sst[:, so:so + 256], Sst[:, so:so + 256],
                                 eGCs[:, j:j + 1], None, ALU.mult),
                 reads=["Sst", K_["eGCs"], "Sbf"], writes=["Sst"])
            P.op("dve", lambda E, so=so:
                 E.tensor_tensor(Sst[:, so:so + 256], Sst[:, so:so + 256], o2b[:, 0:256], ALU.add),
                 reads=["Sst", "o2b"], writes=["Sst"])
            P.op("act", lambda E, so=so:
                 E.copy(Sbf[:, so:so + 256], Sst[:, so:so + 256]),
                 reads=["Sst"], writes=["Sbf"])
        if ATTN_SUB < 5:
            return
        def rec_A(h):
            j, pr, hp = h // 4, 32 * (h % 4), h % 2
            R1, R2 = 4 + 2 * hp, 5 + 2 * hp
            tp = (pr, 0)
            def mmz(E, R1=R1):
                return E.matmul(pb[R1][:, 0:128], consts["zero32"][:, :], consts["zero32"][:, :],
                                start=True, stop=True)
            P.op("pe", mmz, reads=["k_zero32", PBK[R1]], writes=[PBK[R1]])
            def mmod(E, j=j, pr=pr, R1=R1, tp=tp):
                m = None
                for I in (1, 2, 3):
                    w_ = 32 * I
                    m = E.matmul(pb[R1][0:w_, 32 * I:32 * I + 32],
                                 kod[I][pr:pr + 32, w_ * j:w_ * j + w_],
                                 rodU[pr:pr + 32, 128 * j + w_:128 * j + w_ + 32],
                                 start=False, stop=True, skip_group_check=True, tile_position=tp)
                return m
            P.op("pe", mmod, reads=[K_["kod1"], K_["kod2"], K_["kod3"], K_["rodU"]], writes=[PBK[R1]])
            P.op("pe", lambda E, j=j, pr=pr, R2=R2, tp=tp:
                 E.matmul(pb[R2][:, 0:128], kd[pr:pr + 32, 128 * j:128 * (j + 1)],
                          rd[pr:pr + 32, 128 * j:128 * (j + 1)], start=True, stop=True,
                          tile_position=tp),
                 reads=[K_["kd"], K_["rd"], PBK[R2]], writes=[PBK[R2]])
        def rec_evac(h):
            hp, hq = h % 2, h % 4
            R1, R2 = 4 + 2 * hp, 5 + 2 * hp
            P.op("dve", lambda E, hq=hq, R2=R2:
                 E.tensor_tensor(At[hq][:, :], pb[R2][:, 0:128], consts["mask2"][:, :], ALU.mult),
                 reads=[PBK[R2], "k_mask2", f"At{hq}"], writes=[f"At{hq}"])
            P.op("dve", lambda E, hq=hq, R1=R1:
                 E.tensor_tensor(At[hq][:, :], At[hq][:, :], pb[R1][:, 0:128], ALU.add),
                 reads=[f"At{hq}", PBK[R1]], writes=[f"At{hq}"])
            P.op("act", lambda E, hp=hp, h=h:
                 E.activation(Du[hp][:, :], consts["ident"][:, :], AF.Copy, scale=uS[:, h:h + 1]),
                 reads=[K_["uS"], "k_ident", f"Du{hp}"], writes=[f"Du{hp}"])
        def rec_O(h):
            hq = h % 4
            ob, oc = h // 8, 64 * (h % 8)
            olast = (h % 8 == 7)
            P.op("pe", lambda E, hq=hq, ob=ob, oc=oc, h=h, sl=sl:
                 E.matmul(pb[ob][:, oc:oc + 64], At[hq][:, :], vT[sl][:, 64 * h:64 * h + 64],
                          start=False, stop=False, skip_group_check=True),
                 reads=[f"At{hq}", f"vT{sl}", PBK[ob]], writes=[PBK[ob]])
            hp = h % 2
            P.op("pe", lambda E, hp=hp, ob=ob, oc=oc, h=h, olast=olast, sl=sl:
                 E.matmul(pb[ob][:, oc:oc + 64], Du[hp][:, :], vT[sl][:, 64 * h:64 * h + 64],
                          start=False, stop=olast, skip_group_check=not olast),
                 reads=[f"Du{hp}", f"vT{sl}", PBK[ob]], writes=[PBK[ob]])
        rec_A(0)
        for h in range(1, NH):
            rec_A(h)
            rec_evac(h - 1)
            rec_O(h - 1)
        rec_evac(NH - 1)
        rec_O(NH - 1)
        if ATTN_SUB < 6:
            return
        for b in range(4):
            P.op("dve", lambda E, b=b:
                 E.tensor_reduce(st1[:, 8 * b:8 * b + 8],
                                 pb[b][:, 0:512].rearrange("p (h v) -> p h v", h=8),
                                 AX.X, ALU.add),
                 reads=[PBK[b], "st1"], writes=["st1"])
        for b in range(4):
            P.op("act", lambda E, b=b: E.activation(o2b[:, 512 * b:512 * (b + 1)], pb[b][:, 0:512],
                                                    AF.Square),
                 reads=[PBK[b], "o2b"], writes=["o2b"])
        for b in range(4):
            P.op("dve", lambda E, b=b:
                 E.tensor_reduce(st2[:, 8 * b:8 * b + 8],
                                 o2b[:, 512 * b:512 * (b + 1)].rearrange("p (h v) -> p h v", h=8),
                                 AX.X, ALU.add),
                 reads=["o2b", "st2"], writes=["st2"])
        P.op("dve", lambda E: E.tensor_scalar(stm[:, :], st1[:, :], 1.0 / HV, None, ALU.mult),
             reads=["st1", "stm"], writes=["stm"])
        P.op("dve", lambda E: E.tensor_tensor(stv[:, :], stm[:, :], stm[:, :], ALU.mult),
             reads=["stm", "stv"], writes=["stv"])
        P.op("dve", lambda E: E.scalar_tensor_tensor(stv[:, :], st2[:, :], 1.0 / HV, stv[:, :],
                                                     ALU.mult, ALU.subtract),
             reads=["st2", "stv"], writes=["stv"])
        P.op("act", lambda E: E.activation(stv[:, :], stv[:, :], AF.Ln, bias=csc[:, 0:1]),
             reads=["stv", "csc"], writes=["stv"])
        P.op("act", lambda E: E.activation(str_[:, :], stv[:, :], AF.Exp, scale=csc[:, 1:2]),
             reads=["stv", "csc"], writes=["str_"])
        for b in range(4):
            P.op("dve", lambda E, b=b:
                 E.tensor_tensor(o2b[:, 512 * b:512 * (b + 1)], pb[b][:, 0:512],
                                 bcast(stm[:, 8 * b:8 * b + 8], 64), ALU.subtract),
                 reads=[PBK[b], "stm", "o2b"], writes=["o2b"])
        for b in range(4):
            P.op("dve", lambda E, b=b:
                 E.tensor_tensor(onb[:, 512 * b:512 * (b + 1)], o2b[:, 512 * b:512 * (b + 1)],
                                 bcast(str_[:, 8 * b:8 * b + 8], 64), ALU.mult),
                 reads=["o2b", "str_", "onb"], writes=["onb"])
        def rec_ot(v16):
            bk = 5 + (v16 % 2)
            bview = pb[bk][:, 64:128].bitcast(BF16)
            P.op("pe", lambda E, v16=v16, bview=bview:
                 E.transpose(bview, onb[:, 128 * v16:128 * (v16 + 1)], consts["ident"][:, :]),
                 reads=["onb", "k_ident", PBK[bk]], writes=[PBK[bk]])
        def rec_ote(v16):
            bk = 5 + (v16 % 2)
            bview = pb[bk][:, 64:128].bitcast(BF16)
            P.op("act", lambda E, v16=v16, bview=bview:
                 E.copy(onT[:, 128 * v16:128 * (v16 + 1)], bview),
                 reads=[PBK[bk], ("onT", v16 // 4)], writes=[("onT", v16 // 4)])
        def rec_ogq(q):
            cq = slice(512 * q, 512 * (q + 1))
            P.op("dve", lambda E, q=q, cq=cq:
                 E.tensor_tensor(onT[:, cq], onT[:, cq],
                                 bcast(vt["gnw"][:, 4 * q:4 * q + 4], 128), ALU.mult),
                 reads=[("onT", q), "vt_gnw"], writes=[("onT", q)])
            P.op("dve", lambda E, q=q, cq=cq:
                 E.tensor_tensor(onT[:, cq], onT[:, cq],
                                 bcast(vt["gnb"][:, 4 * q:4 * q + 4], 128), ALU.add),
                 reads=[("onT", q), "vt_gnb"], writes=[("onT", q)])
            ogv = bass.AP(og.tensor, og.offset + 1024 * q + 128 * (ch % 2),
                          [list(og.ap[0]), [256, 4], [1, 128]])
            P.op("dve", lambda E, q=q, cq=cq, sl=sl, ogv=ogv:
                 E.tensor_tensor(ogv, onT[:, cq], sgF[sl][:, cq], ALU.mult),
                 reads=[("onT", q), f"sgF{sl}", ("og", q)], writes=[("og", q)])
        rec_ot(0)
        for v16 in range(1, 16):
            rec_ot(v16)
            rec_ote(v16 - 1)
            if v16 % 4 == 0:
                rec_ogq(v16 // 4 - 1)
        rec_ote(15)
        rec_ogq(3)

    def phase_c(chpair):
        t0 = 256 * chpair
        for tt in range(2):
            P.dma("sp", "xr" if tt == 0 else "xo", lambda E, tt=tt, t0=t0:
                  E.dma_start(sumb2[tt][:, :], xTok[t0 + 128 * tt:t0 + 128 * (tt + 1), :]),
                  reads=[f"sumb{tt}"], writes=[f"sumb{tt}"])
        for kv in range(16):
            wsl = kv % 2
            P.dma("sp", f"f{wsl}", lambda E, kv=kv, wsl=wsl:
                  E.dma_start(wob[wsl], wc["wo"][128 * kv:128 * (kv + 1), :]),
                  reads=[("wc", "wo"), WOBK[wsl]], writes=[WOBK[wsl]])
            for tt in range(2):
                for q in range(4):
                    bank = 4 * tt + q
                    P.op("pe", lambda E, kv=kv, tt=tt, q=q, bank=bank, wsl=wsl:
                         E.matmul(pb[bank][:, 0:512], og[:, 256 * kv + 128 * tt:256 * kv + 128 * (tt + 1)],
                                  wob[wsl][:, 512 * q:512 * (q + 1)],
                                  start=(kv == 0), stop=(kv == 15)),
                         reads=[WOBK[wsl], ("og", kv // 4), PBK[bank]], writes=[PBK[bank]])
        for tt in range(2):
            for q in range(4):
                bank = 4 * tt + q
                P.op("dve", lambda E, tt=tt, q=q, bank=bank:
                     E.tensor_tensor(sumb2[tt][:, 512 * q:512 * (q + 1)], sumb2[tt][:, 512 * q:512 * (q + 1)],
                                     pb[bank][:, 0:512], ALU.add),
                     reads=[PBK[bank], f"sumb{tt}"], writes=[f"sumb{tt}"])
            P.op("dve", lambda E: E.memset(lnt[:, 0:16], 0.0), writes=["lnt"])
            P.op("dve", lambda E, tt=tt: E.tensor_reduce(lnt[:, 0:1], sumb2[tt][:, :], AX.X, ALU.add),
                 reads=[f"sumb{tt}", "lnt"], writes=["lnt"])
            for q in range(4):
                P.op("act", lambda E, q=q, tt=tt: E.activation(o2b[:, 512 * q:512 * (q + 1)],
                                                               sumb2[tt][:, 512 * q:512 * (q + 1)], AF.Square,
                                                               accum_out=lnt[:, 1 + q:2 + q]),
                     reads=[f"sumb{tt}", "lnt", "o2b"], writes=["o2b", "lnt"])
            P.op("dve", lambda E: E.tensor_reduce(lnt[:, 5:6], lnt[:, 1:5], AX.X, ALU.add),
                 reads=["lnt"], writes=["lnt"])
            P.op("dve", lambda E: E.tensor_scalar(lnt[:, 6:7], lnt[:, 0:1], 1.0 / D, None, ALU.mult),
                 reads=["lnt"], writes=["lnt"])
            P.op("dve", lambda E: E.tensor_tensor(lnt[:, 7:8], lnt[:, 6:7], lnt[:, 6:7], ALU.mult),
                 reads=["lnt"], writes=["lnt"])
            P.op("dve", lambda E: E.scalar_tensor_tensor(lnt[:, 8:9], lnt[:, 5:6], 1.0 / D,
                                                         lnt[:, 7:8], ALU.mult, ALU.subtract),
                 reads=["lnt"], writes=["lnt"])
            P.op("act", lambda E: E.activation(lnt[:, 9:10], lnt[:, 8:9], AF.Ln, bias=csc[:, 0:1]),
                 reads=["lnt", "csc"], writes=["lnt"])
            P.op("act", lambda E: E.activation(lnt[:, 10:11], lnt[:, 9:10], AF.Exp, scale=csc[:, 1:2]),
                 reads=["lnt", "csc"], writes=["lnt"])
            P.op("dve", lambda E: E.tensor_tensor(lnt[:, 11:12], lnt[:, 6:7], lnt[:, 10:11], ALU.mult),
                 reads=["lnt"], writes=["lnt"])
            P.op("dve", lambda E: E.tensor_scalar(lnt[:, 12:13], lnt[:, 11:12], -1.0, None, ALU.mult),
                 reads=["lnt"], writes=["lnt"])
            P.op("dve", lambda E, tt=tt: E.tensor_scalar(x1v[tt], sumb2[tt][:, :],
                                                         lnt[:, 10:11], lnt[:, 12:13],
                                                         ALU.mult, ALU.add),
                 reads=[f"sumb{tt}", "lnt", X1K[tt]], writes=[X1K[tt]])
            P.dma("act", "x1q", lambda E, tt=tt, t0=t0:
                  E.dma_start(x1_s[t0 + 128 * tt:t0 + 128 * (tt + 1), :], x1v[tt]),
                  reads=[X1K[tt]], writes=[("x1w", t0, tt)])

    NCHx = min(NCH, ATTN_NCH) if ATTN_NCH else NCH
    load_chunk_streams(0)
    for ch in range(NCHx):
        if ch + 1 < NCHx:
            load_chunk_streams(ch + 1)
        attn_chunk(ch)
        if ch % 2 == 1 and ATTN_SUB >= 7:
            phase_c(ch // 2)

    if stage == "attn":
        P.barrier()
        for s_ in P.dma_streams:
            if P.dma_cnt[s_]:
                P.barrier_wait("sp", s_, P.dma_total(s_))
        P.op("dve", lambda E: E.memset(sumb2[0][0:1, 0:4], 0.0), writes=["sumb0"])
        P.dma("sp", "outq", lambda E: E.dma_start(out[0:1, 0:4], sumb2[0][0:1, 0:4]), reads=["sumb0"])
        P.barrier_wait("sp", "outq", P.dma_total("outq"))
        P.build()
        stack.close()
        return nc, dbg_out

    P.barrier()
    AR.reset()
    x1T = AR.bf(16 * FB)
    gu = AR.bf(44 * FB)
    gwb = [AR.bf(2048) for _ in range(3)]
    uwb = [AR.bf(2048) for _ in range(3)]
    fdw = [AR.bf(2048) for _ in range(3)]
    gtmp = [AR.bf(512) for _ in range(3)]
    x1k = [AR.bf(2048) for _ in range(2)]
    sumf = AR.f32(2048)
    sqf = AR.f32(2048)
    _ob = AR.f32(2048)
    outb = [_ob, _ob]
    lnf = AR.f32(32)
    NMH = FH // 128

    def ffn_w_load(cache, src_int, mh, dstbuf, dstkey, ldst, wrst, first):
        if first:
            sl = dstkey
            isl = mh % 2
            P.dma("sp", ldst, lambda E, mh=mh, isl=isl:
                  E.dma_start(ldf[isl][:, :].bitcast(I32), src_int[mh, :, :]),
                  writes=[f"ldf{isl}"])
            P.op("act", lambda E, isl=isl, dstbuf=dstbuf:
                 E.activation(dstbuf[:, :], ldf[isl][:, :].bitcast(I32), AF.Copy, bias=-128.0),
                 reads=[f"ldf{isl}"], writes=[dstkey])
            P.dma("act", wrst,
                  lambda E, mh=mh, dstbuf=dstbuf: E.dma_start(cache[mh, :, :], dstbuf[:, :]),
                  reads=[dstkey], writes=[("fcache", id(cache), mh)])
        else:
            P.dma("sp", ldst, lambda E, mh=mh, dstbuf=dstbuf:
                  E.dma_start(dstbuf[:, :], cache[mh, :, :]),
                  reads=[("fcache", id(cache), mh)], writes=[dstkey])

    for fb in range(NFB):
        t0f = fb * FB
        x1keys = [("x1w", 256 * p, tt) for p in range(t0f // 256, (t0f + FB) // 256)
                  for tt in range(2)]
        for dt in range(16):
            strm = "f0" if dt % 2 == 0 else "f1"
            P.dma("sp", strm, lambda E, dt=dt, t0f=t0f:
                  E.dma_start(x1T[:, FB * dt:FB * (dt + 1)],
                              x1_s[t0f:t0f + FB, 128 * dt:128 * (dt + 1)], transpose=True),
                  reads=x1keys, writes=["x1T"])
        for mh in range(NMH):
            gsl = mh % 3
            first = (fb == 0)
            ffn_w_load(fg_s, wq_gT, mh, gwb[gsl], f"gwb{gsl}", f"fg{gsl}", "fw0", first)
            ffn_w_load(fu_s, wq_uT, mh, uwb[gsl], f"uwb{gsl}", f"fu{gsl}", "fw1", first)
            gb, ub = 2 * (mh % 2), 2 * (mh % 2) + 1
            def mmg(E, mh=mh, gsl=gsl, gb=gb):
                for kt in range(16):
                    m = E.matmul(pb[gb][:, 0:FB], gwb[gsl][:, 128 * kt:128 * (kt + 1)],
                                 x1T[:, FB * kt:FB * kt + FB], start=(kt == 0), stop=(kt == 15))
                return m
            P.op("pe", mmg, reads=[f"gwb{gsl}", "x1T"], writes=[PBK[gb]])
            def mmu(E, mh=mh, gsl=gsl, ub=ub):
                for kt in range(16):
                    m = E.matmul(pb[ub][:, 0:FB], uwb[gsl][:, 128 * kt:128 * (kt + 1)],
                                 x1T[:, FB * kt:FB * kt + FB], start=(kt == 0), stop=(kt == 15))
                return m
            P.op("pe", mmu, reads=[f"uwb{gsl}", "x1T"], writes=[PBK[ub]])
            P.op("act", lambda E, mh=mh, gsl=gsl, gb=gb:
                 E.activation(gtmp[gsl][:, 0:FB], pb[gb][:, 0:FB], AF.Silu,
                              scale=vt["sg"][:, mh:mh + 1]),
                 reads=[PBK[gb], "vt_sg"], writes=[f"gtmp{gsl}"])
            P.op("dve", lambda E, mh=mh, gsl=gsl, ub=ub:
                 E.scalar_tensor_tensor(gu[:, FB * mh:FB * (mh + 1)], pb[ub][:, 0:FB],
                                        vt["su"][:, mh:mh + 1], gtmp[gsl][:, 0:FB],
                                        ALU.mult, ALU.mult),
                 reads=[PBK[ub], "vt_su", f"gtmp{gsl}"], writes=["gu"])
        for grp in range(FB // 256):
            for mh in range(NMH):
                dsl = mh % 3
                ffn_w_load(fd_s, wq_dT, mh, fdw[dsl], f"fdw{dsl}", f"fd{dsl}", "fw2",
                           (fb == 0 and grp == 0))
                def mmd(E, mh=mh, grp=grp, dsl=dsl):
                    m = None
                    for tt2 in range(2):
                        for q in range(4):
                            m = E.matmul(pb[4 * tt2 + q][:, 0:512],
                                         gu[:, FB * mh + 256 * grp + 128 * tt2:FB * mh + 256 * grp + 128 * (tt2 + 1)],
                                         fdw[dsl][:, 512 * q:512 * (q + 1)],
                                         start=(mh == 0), stop=(mh == NMH - 1),
                                         skip_group_check=(mh not in (0, NMH - 1)))
                    return m
                P.op("pe", mmd, reads=[f"fdw{dsl}", "gu"], writes=[PBK[0], PBK[1], PBK[2], PBK[3],
                                                                   PBK[4], PBK[5], PBK[6], PBK[7]])
            for tt2 in range(2):
                tg = t0f + 256 * grp + 128 * tt2
                ksl = tt2 % 2
                P.dma("sp", "xr" if ksl == 0 else "xo", lambda E, tg=tg, ksl=ksl:
                      E.dma_start(x1k[ksl][:, :], x1_s[tg:tg + 128, :]),
                      reads=x1keys, writes=[f"x1k{ksl}"])

                for q in range(4):
                    bank = 4 * tt2 + q
                    P.op("dve", lambda E, q=q, bank=bank:
                         E.tensor_tensor(sumf[:, 512 * q:512 * (q + 1)], pb[bank][:, 0:512],
                                         sdb[:, 512 * q:512 * (q + 1)], ALU.mult),
                         reads=[PBK[bank], "sdb"], writes=["sumf"])
                for q in range(4):
                    P.op("dve", lambda E, q=q, ksl=ksl:
                         E.tensor_tensor(sumf[:, 512 * q:512 * (q + 1)], sumf[:, 512 * q:512 * (q + 1)],
                                         x1k[ksl][:, 512 * q:512 * (q + 1)], ALU.add),
                         reads=["sumf", f"x1k{ksl}"], writes=["sumf"])
                P.op("dve", lambda E: E.memset(lnf[:, 0:16], 0.0), writes=["lnf"])
                P.op("dve", lambda E: E.tensor_reduce(lnf[:, 0:1], sumf[:, :], AX.X, ALU.add),
                     reads=["sumf", "lnf"], writes=["lnf"])
                for q in range(4):
                    P.op("act", lambda E, q=q: E.activation(sqf[:, 512 * q:512 * (q + 1)],
                                                            sumf[:, 512 * q:512 * (q + 1)], AF.Square,
                                                            accum_out=lnf[:, 1 + q:2 + q]),
                         reads=["sumf", "lnf"], writes=["sqf", "lnf"])
                P.op("dve", lambda E: E.tensor_reduce(lnf[:, 5:6], lnf[:, 1:5], AX.X, ALU.add),
                     reads=["lnf"], writes=["lnf"])
                P.op("dve", lambda E: E.tensor_scalar(lnf[:, 6:7], lnf[:, 0:1], 1.0 / D, None, ALU.mult),
                     reads=["lnf"], writes=["lnf"])
                P.op("dve", lambda E: E.tensor_tensor(lnf[:, 7:8], lnf[:, 6:7], lnf[:, 6:7], ALU.mult),
                     reads=["lnf"], writes=["lnf"])
                P.op("dve", lambda E: E.scalar_tensor_tensor(lnf[:, 8:9], lnf[:, 5:6], 1.0 / D,
                                                             lnf[:, 7:8], ALU.mult, ALU.subtract),
                     reads=["lnf"], writes=["lnf"])
                P.op("act", lambda E: E.activation(lnf[:, 9:10], lnf[:, 8:9], AF.Ln, bias=csc[:, 0:1]),
                     reads=["lnf", "csc"], writes=["lnf"])
                P.op("act", lambda E: E.activation(lnf[:, 10:11], lnf[:, 9:10], AF.Exp, scale=csc[:, 1:2]),
                     reads=["lnf", "csc"], writes=["lnf"])
                P.op("dve", lambda E: E.tensor_tensor(lnf[:, 11:12], lnf[:, 6:7], lnf[:, 10:11], ALU.mult),
                     reads=["lnf"], writes=["lnf"])
                P.op("dve", lambda E: E.tensor_scalar(lnf[:, 12:13], lnf[:, 11:12], -1.0, None, ALU.mult),
                     reads=["lnf"], writes=["lnf"])
                P.op("dve", lambda E, tt2=tt2: E.tensor_scalar(outb[tt2][:, :], sumf[:, :],
                                                               lnf[:, 10:11], lnf[:, 12:13],
                                                               ALU.mult, ALU.add),
                     reads=["sumf", "lnf"], writes=["outb0"])
                P.dma("act", "outq", lambda E, tt2=tt2, tg=tg:
                      E.dma_start(out[tg:tg + 128, :], outb[tt2][:, :]),
                      reads=["outb0"], writes=[("outw", tg)])

    P.barrier_wait("sp", "outq", P.dma_total("outq"))
    P.build()
    stack.close()
    return nc, dbg_out


def make_inputs_per_core(b, ins, T=2048):
    x = np.asarray(ins["L0_attn_hidden_states"][b])
    m = {}
    m["x"] = np.ascontiguousarray(x)
    m["xT"] = np.ascontiguousarray(x.T)
    m["Wx1"] = np.asarray(ins["L0_attn_Wx1"])
    m["Wx2"] = np.ascontiguousarray(np.asarray(ins["L0_attn_Wx2"]).reshape(5 * RLO, D))
    m["Wr"] = np.asarray(ins["L0_attn_Wr"])
    m["Ww1"] = np.asarray(ins["L0_attn_Ww1"])
    m["Ww2"] = np.asarray(ins["L0_attn_Ww2"])
    m["bw"] = np.asarray(ins["L0_attn_bw"]).reshape(1, DK)
    m["Wk"] = np.asarray(ins["L0_attn_Wk"])
    m["Wv"] = np.asarray(ins["L0_attn_Wv"])
    m["Wg"] = np.asarray(ins["L0_attn_Wg"])
    m["Wo"] = np.asarray(ins["L0_attn_Wo"])
    def col_img(w):
        wt = np.asarray(w).T
        return np.ascontiguousarray(
            wt.reshape(16, 128, FH // 128, 128).transpose(2, 1, 0, 3).reshape(FH // 128, 128, D))
    m["wq_gT"] = col_img(ins["L0_ffn_wq_gate"])
    m["wq_uT"] = col_img(ins["L0_ffn_wq_up"])
    m["wq_dT"] = np.ascontiguousarray(
        np.asarray(ins["L0_ffn_wq_down"]).T.reshape(FH // 128, 128, D))
    m["s_d"] = np.asarray(ins["L0_ffn_s_down"]).reshape(1, D)

    def tilecol(v, nt):
        return np.ascontiguousarray(np.asarray(v).reshape(nt, 128).T)
    m["vt_mu"] = tilecol(ins["L0_attn_mu_x"], 16)
    m["vt_bonus"] = tilecol(np.asarray(ins["L0_attn_bonus"]).reshape(NH * HK), 8)
    m["vt_gnw"] = tilecol(np.tile(np.asarray(ins["L0_attn_gn_w"]), NH), 16)
    m["vt_gnb"] = tilecol(np.tile(np.asarray(ins["L0_attn_gn_b"]), NH), 16)
    m["vt_sg"] = tilecol(ins["L0_ffn_s_gate"], 44)
    m["vt_su"] = tilecol(ins["L0_ffn_s_up"], 44)
    for k, v in host_constants().items():
        m["c_" + k] = v
    return m


def kernel(**inputs):
    T = 2048
    nc, _ = build_program(T=T, stage="full")
    in_maps = [make_inputs_per_core(b, inputs, T) for b in range(BATCH)]
    res = run_bass_kernel_spmd(nc, in_maps, core_ids=list(range(BATCH)))
    return np.stack([r["out"] for r in res.results], axis=0)


# version 20
# speedup vs baseline: 1.0618x; 1.0618x over previous
from collections import defaultdict
from contextlib import ExitStack

import numpy as np
import ml_dtypes
import concourse.bass as bass
import concourse.mybir as mybir
from concourse.alu_op_type import AluOpType as ALU
from concourse.bass_utils import run_bass_kernel_spmd

F32 = mybir.dt.float32
BF16 = mybir.dt.bfloat16
I32 = mybir.dt.int32
AF = mybir.ActivationFunctionType
AX = mybir.AxisListType

D, DK, DV, NH, HK, HV = 2048, 1024, 2048, 32, 32, 64
RLO, GLO, FH, EPS = 32, 64, 5632, 1e-5
BATCH = 8
ATTN_SUB = 99
ATTN_NCH = None
GN_SUB = 99
C = 128
NB = 32
ACLAMP = 40.0


class Prog:
    ENGINES = ("pe", "act", "dve", "pool", "sp")

    def __init__(self, nc, dma_streams=()):
        self.nc = nc
        self.dma_streams = list(dma_streams)
        self.ops = []
        self.cnt = {e: 0 for e in self.ENGINES}
        self.dma_cnt = {s: 0 for s in self.dma_streams}
        self.clock = {e: defaultdict(int) for e in self.ENGINES}
        self.last_w = {}
        self.readers = defaultdict(dict)
        self.waited = defaultdict(set)

    def _dep_waits(self, eng, reads, writes):
        waits = []

        def need(sem_key, val):
            if sem_key in self.dma_cnt:
                val = 16 * self.dma_cnt[sem_key]
            if self.clock[eng][sem_key] >= val:
                return
            self.clock[eng][sem_key] = val
            for i, (sk, _v) in enumerate(waits):
                if sk == sem_key:
                    waits[i] = (sem_key, val)
                    return
            waits.append((sem_key, val))

        for b in reads:
            w = self.last_w.get(b)
            if w is not None:
                need(*w)
        for b in writes:
            w = self.last_w.get(b)
            if w is not None:
                need(*w)
            for sk, v in self.readers.get(b, {}).items():
                need(sk, v)
        return waits

    def op(self, eng, emit, reads=(), writes=()):
        waits = self._dep_waits(eng, reads, writes)
        self.cnt[eng] += 1
        val = self.cnt[eng]
        self.ops.append((eng, waits, emit, eng, val))
        for b in reads:
            self.readers[b][eng] = val
        for b in writes:
            self.last_w[b] = (eng, val)
            self.readers[b] = {}
        for sk, v in waits:
            self.waited[sk].add(v)
        return val

    def dma(self, queue, stream, emit, reads=(), writes=()):
        waits = self._dep_waits(queue, reads, writes)
        self.dma_cnt[stream] += 1
        val = 16 * self.dma_cnt[stream]
        self.ops.append((queue, waits, emit, stream, val))
        for b in reads:
            self.readers[b][stream] = val
        for b in writes:
            self.last_w[b] = (stream, val)
            self.readers[b] = {}
        for sk, v in waits:
            self.waited[sk].add(v)
        return val

    def barrier(self):
        snap_eng = {e: self.cnt[e] for e in self.ENGINES}
        snap_dma = dict(self.dma_cnt)
        for e in self.ENGINES:
            waits = []
            for e2, v in snap_eng.items():
                if e2 != e and v and self.clock[e][e2] < v:
                    self.clock[e][e2] = v
                    waits.append((e2, v))
                    self.waited[e2].add(v)
            for s, n in snap_dma.items():
                if n and self.clock[e][s] < 16 * n:
                    self.clock[e][s] = 16 * n
                    waits.append((s, 16 * n))
            if waits:
                self.ops.append((e, waits, None, None, 0))

    def barrier_wait(self, eng, sem_key, val):
        if self.clock[eng][sem_key] >= val:
            return
        self.clock[eng][sem_key] = val
        self.ops.append((eng, [(sem_key, val)], None, None, 0))
        self.waited[sem_key].add(val)

    def dma_total(self, stream):
        return 16 * self.dma_cnt[stream]

    def build(self):
        nc = self.nc
        remap, inc_needed = {}, {}
        for e in self.ENGINES:
            need = set(self.waited.get(e, ()))
            inc_needed[e] = need
            remap[e] = {v: i + 1 for i, v in enumerate(sorted(need))}

        def mapped(sk, v):
            return remap[sk][v] if sk in remap else v

        with ExitStack() as stack:
            sems = {}
            for e in self.ENGINES:
                if self.cnt[e] or self.waited.get(e):
                    sems[e] = stack.enter_context(nc.semaphore("sem_" + e))
            for s in self.dma_streams:
                if self.dma_cnt[s]:
                    sems[s] = stack.enter_context(nc.semaphore("dma_" + s))
            block = stack.enter_context(nc.Block())
            per = {e: [] for e in self.ENGINES}
            for rec in self.ops:
                per[rec[0]].append(rec)

            def run(engine_obj, eng_name):
                for (_e, waits, emit, inc_key, val) in per[eng_name]:
                    for sk, v in waits:
                        engine_obj.wait_ge(sems[sk], mapped(sk, v))
                    if emit is None:
                        continue
                    inst = emit(engine_obj)
                    if inc_key in remap:
                        if val in inc_needed[inc_key]:
                            inst.then_inc(sems[inc_key], 1)
                    else:
                        inst.then_inc(sems[inc_key], 16)

            if per["sp"]:
                @block.sync
                def _(e):
                    run(e, "sp")
            if per["pe"]:
                @block.tensor
                def _(e):
                    run(e, "pe")
            if per["act"]:
                @block.scalar
                def _(e):
                    run(e, "act")
            if per["dve"]:
                @block.vector
                def _(e):
                    run(e, "dve")
            if per["pool"]:
                @block.gpsimd
                def _(e):
                    run(e, "pool")


class Arena:

    def __init__(self, tensor, cols):
        self.t = tensor
        self.cols = cols
        self.off = 0

    def reset(self):
        self.off = 0

    def alloc(self, ncols_bf16):
        ncols_bf16 = (ncols_bf16 + 63) & ~63
        o = self.off
        self.off += ncols_bf16
        assert self.off <= self.cols, f"arena overflow: {self.off} > {self.cols}"
        return o

    def bf(self, ncols):
        o = self.alloc(ncols)
        return self.t[:, o:o + ncols]

    def f32(self, ncols):
        o = self.alloc(2 * ncols)
        return self.t[:, o:o + 2 * ncols].bitcast(F32)


def bcast(sl, inner):
    pat = [list(p) for p in sl.ap] + [[0, inner]]
    return bass.AP(sl.tensor, sl.offset, pat)


def host_constants():
    c = {}
    j = np.arange(128)[:, None]
    i = np.arange(128)[None, :]
    c["ntri_excl"] = -(j < i).astype(np.float32)
    c["ntri_incl"] = -(j <= i).astype(np.float32)
    c["mask2"] = (((j // NB) == (i // NB)) & (j < i)).astype(ml_dtypes.bfloat16)
    c["ident"] = np.eye(128).astype(ml_dtypes.bfloat16)
    c["zero32"] = np.zeros((32, 128), dtype=ml_dtypes.bfloat16)
    c["onesf"] = np.ones((1, 128), dtype=np.float32)
    bd = np.zeros((128, 256), dtype=ml_dtypes.bfloat16)
    for h4 in range(4):
        bd[32 * h4:32 * h4 + 32, 64 * h4:64 * h4 + 64] = 1.0
    c["bdmask"] = bd
    sel = np.zeros((128, 4), dtype=ml_dtypes.bfloat16)
    for h4 in range(4):
        sel[32 * h4:32 * h4 + 32, h4] = 1.0
    c["sel4"] = sel
    return c


CONST_SHAPES = {k: (list(v.shape), F32 if v.dtype == np.float32 else BF16)
                for k, v in host_constants().items()}


def build_program(T=2048, stage="full", dbg=()):
    NCH = T // C
    TB = min(512, T)
    NTB = T // TB
    FB = min(512, T)
    NFB = T // FB
    nc = bass.Bass("TRN2", target_bir_lowering=False)

    def din(name, shape, dt=F32):
        return nc.dram_tensor(name, shape, dt, kind="ExternalInput").ap()

    xTok = din("x", [T, D])
    xFt = din("xT", [D, T])
    Wx1 = din("Wx1", [D, 5 * RLO])
    Wx2 = din("Wx2", [5 * RLO, D])
    Wr = din("Wr", [D, DK])
    Ww1 = din("Ww1", [D, GLO])
    Ww2 = din("Ww2", [GLO, DK])
    bw = din("bw", [1, DK])
    Wk = din("Wk", [D, DK])
    Wv = din("Wv", [D, DV])
    Wg = din("Wg", [D, DV])
    Wo = din("Wo", [DV, D])
    wq_gT = din("wq_gT", [FH // 128, 128, D], I32)
    wq_uT = din("wq_uT", [FH // 128, 128, D], I32)
    wq_dT = din("wq_dT", [FH // 128, 128, D], I32)

    s_d = din("s_d", [1, D])
    vt_mu = din("vt_mu", [128, 16])
    vt_bonus = din("vt_bonus", [128, 8])
    vt_gnw = din("vt_gnw", [128, 16])
    vt_gnb = din("vt_gnb", [128, 16])
    vt_sg = din("vt_sg", [128, 44])
    vt_su = din("vt_su", [128, 44])
    CH = {k: din("c_" + k, s, dt) for k, (s, dt) in CONST_SHAPES.items()}
    out = nc.dram_tensor("out", [T, D], F32, kind="ExternalOutput").ap()

    dbg_out = {}

    def mkdbg(name, shape, dt=F32):
        if name in dbg:
            dbg_out[name] = nc.dram_tensor("dbg_" + name, shape, dt,
                                           kind="ExternalOutput").ap()
        return dbg_out.get(name)

    def scr(name, shape, dt=BF16):
        if ("S:" + name) in dbg:
            t = nc.dram_tensor(name, shape, dt, kind="ExternalOutput")
            dbg_out["S:" + name] = name
            return t.ap()
        return nc.dram_tensor(name, shape, dt).ap()

    WCOLS = {"wx1": 5 * RLO, "wr": DK, "ww1": GLO, "wk": DK, "wv": DV, "wg": DV, "wo": D}
    wc = {n: scr(n + "_c", [D, c]) for n, c in WCOLS.items()}
    r_s = scr("r_s", [NCH, 128, 8 * C])
    k_s = scr("k_s", [NCH, 128, 8 * C])
    sg_s = scr("sg_s", [NCH, 128, 16 * C])
    v_s = scr("v_s", [T, DV])
    x1_s = scr("x1_s", [T, D])
    fg_s = scr("fg_s", [FH // 128, 128, 2048])
    fu_s = scr("fu_s", [FH // 128, 128, 2048])
    fd_s = scr("fd_s", [FH // 128, 128, 2048])

    P = Prog(nc, dma_streams=["sc", "ldf0", "ldf1", "wc0", "s0", "s1", "s2", "s3",
                              "c0", "c1", "xr", "xo", "f0", "f1", "fw0",
                              "fw1", "fw2", "outq", "r0", "r1", "k0", "k1", "v0", "v1", "wc1",
                              "g0", "g1", "x1q", "fg0", "fg1", "fg2", "fu0", "fu1", "fu2", "fd0", "fd1", "fd2", "wd0", "wd1"])
    stack = ExitStack()
    SB = lambda name, shape, dt=BF16: stack.enter_context(nc.sbuf_tensor(name, shape, dt))
    pb = [stack.enter_context(nc.psum_tensor(f"pb{i}", [128, 512], F32)) for i in range(8)]
    PBK = [f"pb{i}" for i in range(8)]

    consts = {k: SB("k_" + k, s, dt) for k, (s, dt) in CONST_SHAPES.items()}
    vt = {"mu": SB("vt_mu_s", [128, 16], F32), "bonus": SB("vt_bonus_s", [128, 8], F32),
          "gnw": SB("vt_gnw_s", [128, 16], F32), "gnb": SB("vt_gnb_s", [128, 16], F32),
          "sg": SB("vt_sg_s", [128, 44], F32), "su": SB("vt_su_s", [128, 44], F32)}
    sbw = SB("sbw", [1, DK], F32)
    sWw2 = SB("sWw2", [GLO, DK], BF16)
    swx2a = SB("swx2a", [128, D], BF16)
    swx2b = SB("swx2b", [32, D], BF16)
    tanhu = SB("tanhu", [GLO, T], BF16)
    zf = SB("zf", [128, 2 * T], BF16)
    Sst = SB("Sst", [128, 8 * 256], F32)
    Sbf = SB("Sbf", [128, 8 * 256], BF16)
    sdb = SB("sdb", [128, D], F32)
    csc = SB("csc", [128, 2], F32)
    ARENA_COLS = 68352
    AR = Arena(SB("arena", [128, ARENA_COLS], BF16), ARENA_COLS)

    for k in consts:
        P.dma("sp", "sc", lambda E, k=k: E.dma_start(consts[k][:, :], CH[k]), writes=["k_" + k])
    for nm, src in [("mu", vt_mu), ("bonus", vt_bonus), ("gnw", vt_gnw),
                    ("gnb", vt_gnb), ("sg", vt_sg), ("su", vt_su)]:
        P.dma("sp", "sc", lambda E, nm=nm, src=src: E.dma_start(vt[nm][:, :], src), writes=["vt_" + nm])
    P.dma("sp", "sc", lambda E: E.dma_start(sbw[:, :], bw), writes=["sbw"])

    ldf = [SB(f"ldf{i}", [128, 2048], F32) for i in range(2)]
    ldb = [SB(f"ldb{i}", [128, 2048], BF16) for i in range(2)]
    jobs = []
    for name, src in [("wx1", Wx1), ("wr", Wr), ("ww1", Ww1), ("wk", Wk),
                      ("wv", Wv), ("wg", Wg), ("wo", Wo)]:
        cols = WCOLS[name]
        for dt in range(16):
            jobs.append((src[dt * 128:(dt + 1) * 128, :], wc[name][dt * 128:(dt + 1) * 128, :], 128, cols))
    for idx, (sap, dap, rows, cols) in enumerate(jobs):
        sl = idx % 2
        P.dma("sp", f"ldf{sl}", lambda E, sap=sap, rows=rows, cols=cols, sl=sl:
              E.dma_start(ldf[sl][0:rows, 0:cols], sap), writes=[f"ldf{sl}"])
        P.op("act", lambda E, rows=rows, cols=cols, sl=sl:
             E.copy(ldb[sl][0:rows, 0:cols], ldf[sl][0:rows, 0:cols]),
             reads=[f"ldf{sl}"], writes=[f"ldb{sl}"])
        P.dma("act", f"c{sl}", lambda E, dap=dap, rows=rows, cols=cols, sl=sl:
              E.dma_start(dap, ldb[sl][0:rows, 0:cols]), reads=[f"ldb{sl}"], writes=["wcache"])
    P.dma("sp", "ldf0", lambda E: E.dma_start(ldf[0][0:128, 0:D], Wx2[0:128, :]), writes=["ldf0"])
    P.op("act", lambda E: E.copy(swx2a[:, :], ldf[0][0:128, 0:D]), reads=["ldf0"], writes=["swx2a"])
    P.dma("sp", "ldf1", lambda E: E.dma_start(ldf[1][0:32, 0:D], Wx2[128:160, :]), writes=["ldf1"])
    P.op("act", lambda E: E.copy(swx2b[:, :], ldf[1][0:32, 0:D]), reads=["ldf1"], writes=["swx2b"])
    P.dma("sp", "ldf0", lambda E: E.dma_start(ldf[0][0:GLO, 0:DK], Ww2), writes=["ldf0"])
    P.op("act", lambda E: E.copy(sWw2[:, :], ldf[0][0:GLO, 0:DK]), reads=["ldf0"], writes=["sWw2"])
    P.dma("sp", "ldf1", lambda E: E.dma_start(ldf[1][0:1, 0:D], s_d), writes=["ldf1"])
    for q in range(4):
        P.op("pe", lambda E, q=q: E.matmul(pb[q][:, 0:512], consts["onesf"][0:1, :],
                                           ldf[1][0:1, 512 * q:512 * (q + 1)],
                                           start=True, stop=True),
             reads=["ldf1", "k_onesf"], writes=[PBK[q]])
        P.op("dve", lambda E, q=q: E.tensor_copy(sdb[:, 512 * q:512 * (q + 1)], pb[q][:, 0:512]),
             reads=[PBK[q]], writes=["sdb"])
    P.op("dve", lambda E: E.memset(csc[:, 0:1], EPS), writes=["csc"])
    P.op("dve", lambda E: E.memset(csc[:, 1:2], -0.5), writes=["csc"])
    P.op("dve", lambda E: E.memset(zf[:, :], 0.0), writes=["zf"])
    P.op("dve", lambda E: E.memset(Sst[:, :], 0.0), writes=["Sst"])
    P.op("dve", lambda E: E.memset(Sbf[:, :], 0.0), writes=["Sbf"])

    xTb = AR.bf(16 * T)
    xiT2 = [AR.bf(16 * TB), AR.bf(16 * TB)]
    wcur = AR.bf(16 * 1024)
    evb = [AR.bf(512) for _ in range(4)]

    for dt in range(16):
        sl = dt % 2
        P.dma("sp", f"ldf{sl}", lambda E, dt=dt, sl=sl:
              E.dma_start(ldf[sl][:, 0:T], xFt[dt * 128:(dt + 1) * 128, 0:T]), writes=[f"ldf{sl}"])
        P.op("act", lambda E, dt=dt, sl=sl: E.copy(xTb[:, T * dt:T * (dt + 1)], ldf[sl][:, 0:T]),
             reads=[f"ldf{sl}"], writes=["xTb"])

    JOBS = [("z", "wx1", "z"), ("r", "wr", "feat"), ("w", "ww1", "u"),
            ("k", "wk", "feat"), ("v", "wv", "tok"), ("g", "wg", "feat")]
    JCOLS = {"z": 160, "r": DK, "w": GLO, "k": DK, "v": DV, "g": DV}
    JDST = {"r": r_s, "k": k_s, "g": sg_s}

    def load_half(name, c0, cols, hf):
        hkey = f"wcur{hf}"
        off = 512 * hf
        ctot = WCOLS[name]
        for g, queue in ((0, "sp"), (1, "act")):
            strm = f"wc{hf}" if g == 0 else f"wd{hf}"
            P.dma(queue, strm, lambda E, name=name, g=g, c0=c0, cols=cols, off=off, ctot=ctot:
                  E.dma_start(bass.AP(wcur.tensor, wcur.offset + 1024 * 8 * g + off,
                                      [list(wcur.ap[0]), [1024, 8], [1, cols]]),
                              bass.AP(wc[name].tensor, (8 * g * 128) * ctot + c0,
                                      [[ctot, 128], [128 * ctot, 8], [1, cols]])),
                  reads=["wcache"], writes=[hkey])

    def xi_build(tb, j, slot):
        t0 = tb * TB
        xiT = xiT2[slot]
        xkey = f"xiT{slot}"
        if j > 0:
            fi = j - 1
            lhs, lkey, zrow = (swx2a, "swx2a", 32 * fi) if fi < 4 else (swx2b, "swx2b", 0)
        xstr = lambda off: bass.AP(xTb.tensor, xTb.offset + off,
                                   [list(p) for p in xTb.ap[:1]] + [[T, 16], [1, TB]])
        xiv = xiT[:, :].rearrange("p (a b) -> p a b", b=TB)
        if t0 == 0:
            for dt in range(16):
                P.op("dve", lambda E, dt=dt, xiT=xiT:
                     E.tensor_scalar(xiT[:, TB * dt:TB * dt + 1], xTb[:, T * dt:T * dt + 1],
                                     -1.0, None, ALU.mult), reads=["xTb", xkey], writes=[xkey])
                P.op("dve", lambda E, dt=dt, xiT=xiT:
                     E.tensor_tensor(xiT[:, TB * dt + 1:TB * dt + TB],
                                     xTb[:, T * dt:T * dt + TB - 1],
                                     xTb[:, T * dt + 1:T * dt + TB], ALU.subtract),
                     reads=["xTb", xkey], writes=[xkey])
        else:
            P.op("dve", lambda E, t0=t0, xiv=xiv, xstr=xstr:
                 E.tensor_tensor(xiv, xstr(t0 - 1), xstr(t0), ALU.subtract),
                 reads=["xTb", xkey], writes=[xkey])
        if j == 0:
            P.op("dve", lambda E, xiv=xiv:
                 E.tensor_tensor(xiv, xiv, bcast(vt["mu"][:, 0:16], TB), ALU.mult),
                 reads=[xkey, "vt_mu"], writes=[xkey])
        else:
            for dt in range(16):
                bank = 6 + (dt % 2)
                def mmu(E, dt=dt, bank=bank, zrow=zrow, lhs=lhs, t0=t0, fi=fi):
                    zsrc = zf[zrow:zrow + 32, t0:t0 + TB] if fi < 4 else zf[0:32, T + t0:T + t0 + TB]
                    return E.matmul(pb[bank][:, 0:TB],
                                    lhs[zrow:zrow + 32, dt * 128:(dt + 1) * 128] if fi < 4
                                    else lhs[0:32, dt * 128:(dt + 1) * 128],
                                    zsrc, start=True, stop=True, tile_position=(zrow, 0))
                P.op("pe", mmu, reads=[lkey, "zf"], writes=[PBK[bank]])
                P.op("dve", lambda E, dt=dt, bank=bank, xiT=xiT:
                     E.tensor_tensor(xiT[:, TB * dt:TB * dt + TB], xiT[:, TB * dt:TB * dt + TB],
                                     pb[bank][:, 0:TB], ALU.mult),
                     reads=[xkey, PBK[bank]], writes=[xkey])
        P.op("dve", lambda E, t0=t0, xiv=xiv, xstr=xstr:
             E.tensor_tensor(xiv, xiv, xstr(t0), ALU.add),
             reads=[xkey, "xTb"], writes=[xkey])

    def job_mms(tb, j, slot):
        name, wname, kind = JOBS[j]
        t0 = tb * TB
        xiT = xiT2[slot]
        xkey = f"xiT{slot}"
        cols_total = JCOLS[name]
        nbatch = (cols_total + 511) // 512
        out = []
        for bidx in range(nbatch):
            c0 = 512 * bidx
            cols = min(512, cols_total - c0)
            hf = bidx % 2
            hkey = f"wcur{hf}"
            off = 512 * hf
            load_half(wname, c0, cols, hf)
            batch_ops = []
            if kind == "tok":
                for tt in range(TB // 128):
                    bank = 4 + (tt % 2)
                    def mm(E, tt=tt, bank=bank, off=off, xiT=xiT):
                        for kt in range(16):
                            m = E.matmul(pb[bank][:, 0:512],
                                         xiT[:, TB * kt + 128 * tt:TB * kt + 128 * (tt + 1)],
                                         wcur[:, 1024 * kt + off:1024 * kt + off + 512],
                                         start=(kt == 0), stop=(kt == 15))
                        return m
                    P.op("pe", mm, reads=[hkey, xkey], writes=[PBK[bank]])
                    sl = tt % 4
                    P.op("act", lambda E, bank=bank, sl=sl:
                         E.copy(evb[sl][:, 0:512], pb[bank][:, 0:512]),
                         reads=[PBK[bank]], writes=[f"evb{sl}"])
                    P.dma("act", f"s{sl}", lambda E, tt=tt, c0=c0, sl=sl, t0=t0:
                          E.dma_start(v_s[t0 + 128 * tt:t0 + 128 * (tt + 1), c0:c0 + 512],
                                      evb[sl][:, 0:512]),
                          reads=[f"evb{sl}"], writes=[("spillv", t0, tt, c0)])
            else:
                nml = (cols + 127) // 128
                for ml in range(nml):
                    mt = 4 * bidx + ml
                    rows = min(128, cols - 128 * ml)
                    bank = 4 + (ml % 2)
                    def mm(E, ml=ml, bank=bank, rows=rows, off=off, xiT=xiT):
                        for kt in range(16):
                            m = E.matmul(pb[bank][0:rows, 0:TB],
                                         wcur[:, 1024 * kt + off + 128 * ml:1024 * kt + off + 128 * ml + rows],
                                         xiT[:, TB * kt:TB * kt + TB], start=(kt == 0), stop=(kt == 15))
                        return m
                    P.op("pe", mm, reads=[hkey, xkey], writes=[PBK[bank]])
                    if kind == "z":
                        P.op("act", lambda E, mt=mt, rows=rows, bank=bank, t0=t0:
                             E.activation(zf[0:rows, T * mt + t0:T * mt + t0 + TB],
                                          pb[bank][0:rows, 0:TB], AF.Tanh),
                             reads=[PBK[bank]], writes=["zf"])
                    elif kind == "u":
                        P.op("act", lambda E, bank=bank, t0=t0:
                             E.activation(tanhu[0:GLO, t0:t0 + TB], pb[bank][0:GLO, 0:TB], AF.Tanh),
                             reads=[PBK[bank]], writes=["tanhu"])
                    else:
                        dst = JDST[name]
                        sl = mt % 4
                        if name == "g":
                            P.op("act", lambda E, sl=sl, bank=bank:
                                 E.activation(evb[sl][:, 0:TB], pb[bank][:, 0:TB], AF.Silu),
                                 reads=[PBK[bank]], writes=[f"evb{sl}"])
                        else:
                            P.op("act", lambda E, sl=sl, bank=bank:
                                 E.copy(evb[sl][:, 0:TB], pb[bank][:, 0:TB]),
                                 reads=[PBK[bank]], writes=[f"evb{sl}"])
                        for cq in range(TB // C):
                            ch = (t0 // C) + cq
                            P.dma("act", f"s{sl}", lambda E, mt=mt, sl=sl, cq=cq, ch=ch, dst=dst:
                                  E.dma_start(dst[ch, :, C * mt:C * (mt + 1)],
                                              evb[sl][:, C * cq:C * (cq + 1)]),
                                  reads=[f"evb{sl}"], writes=[("spill", id(dst), ch, mt)])
            if bidx == 0 and j + 1 < len(JOBS):
                xi_build(tb, j + 1, (j + 1) % 2)

    def build_proj():
        for tb in range(NTB):
            xi_build(tb, 0, 0)
            for j in range(len(JOBS)):
                job_mms(tb, j, j % 2)

    build_proj()

    if "zf" in dbg:
        d = mkdbg("zf", [128, T])
        P.dma("sp", "outq", lambda E, d=d: E.dma_start(d, zf[:, :].bitcast(F32)), reads=["zf"])
    if "tanhu" in dbg:
        d = mkdbg("tanhu", [GLO, T // 2])
        P.dma("sp", "outq", lambda E, d=d: E.dma_start(d, tanhu[:, :].bitcast(F32)), reads=["tanhu"])

    if stage == "proj":
        P.barrier()
        P.barrier_wait("sp", "outq", P.dma_total("outq"))
        for s in ("s0", "s1", "s2", "s3", "c0", "c1"):
            P.barrier_wait("sp", s, P.dma_total(s))
        P.op("dve", lambda E: E.memset(ldf[0][0:1, 0:4], 0.0), writes=["ldf0"])
        P.dma("sp", "outq", lambda E: E.dma_start(out[0:1, 0:4], ldf[0][0:1, 0:4].bitcast(F32)[0:1, 0:2].bitcast(F32) if False else ldf[0][0:1, 0:4]), reads=["ldf0"])
        P.barrier_wait("sp", "outq", P.dma_total("outq"))
        P.build()
        stack.close()
        return nc, dbg_out

    P.barrier()
    AR.reset()
    rF = [AR.bf(1024) for _ in range(2)]
    kF = [AR.bf(1024) for _ in range(2)]
    sgF = [AR.bf(2048) for _ in range(2)]
    vT = [AR.bf(2048) for _ in range(2)]
    expw2 = [AR.f32(1024) for _ in range(2)]
    Gx2 = [AR.f32(1024) for _ in range(2)]
    Gi2 = [AR.f32(1024) for _ in range(2)]
    tmpf2 = [AR.f32(1024) for _ in range(2)]
    eGC2 = [AR.f32(8) for _ in range(2)]
    kod2 = [[None] + [AR.bf(256 * I) for I in (1, 2, 3)] for _ in range(2)]
    rodU2 = [AR.bf(1024) for _ in range(2)]
    kd2 = [AR.bf(1024) for _ in range(2)]
    rd2 = [AR.bf(1024) for _ in range(2)]
    r0b2 = [AR.bf(1024) for _ in range(2)]
    kCf2 = [AR.bf(1024) for _ in range(2)]
    kCt2 = [AR.bf(1024) for _ in range(2)]
    tmpu2 = [AR.bf(1024) for _ in range(2)]
    uS2 = [AR.f32(32) for _ in range(2)]
    At = [AR.bf(128) for _ in range(4)]
    Bt = [AR.bf(128) for _ in range(2)]
    Du = [AR.bf(128) for _ in range(2)]
    st1 = AR.f32(32)
    st2 = AR.f32(32)
    stm = AR.f32(32)
    stv = AR.f32(32)
    str_ = AR.f32(32)
    o2b = AR.f32(2048)
    onb = AR.bf(2048)
    onT = AR.bf(2048)
    og = AR.bf(4096)
    sumb2 = [AR.f32(2048) for _ in range(2)]
    lnt = AR.f32(32)
    wob = [ldb[0][:, :], ldb[1][:, :]]
    WOBK = ["ldb0", "ldb1"]
    x1v = [ldf[0][:, :].bitcast(BF16)[:, 0:2048], ldf[0][:, :].bitcast(BF16)[:, 2048:4096]]
    X1K = ["ldf0", "ldf0"]

    def load_chunk_streams(ch):
        sl = ch % 2
        for (dst, src, nt, key, strm) in [(rF[sl], r_s, 8, f"rF{sl}", f"r{sl}"),
                                          (kF[sl], k_s, 8, f"kF{sl}", f"k{sl}"),
                                          (sgF[sl], sg_s, 16, f"sgF{sl}", f"g{sl}")]:
            P.dma("sp", strm, lambda E, dst=dst, src=src, nt=nt, ch=ch:
                  E.dma_start(dst[:, 0:128 * nt], src[ch, :, :]),
                  reads=[("spill", id(src), ch, mt) for mt in range(nt)], writes=[key])
        t0v = (ch * C // TB) * TB
        ttv = (ch * C % TB) // 128
        P.dma("sp", f"v{sl}", lambda E, ch=ch:
              E.dma_start(vT[sl][:, :], v_s[ch * C:(ch + 1) * C, :]),
              reads=[("spillv", t0v, ttv, c0) for c0 in range(0, DV, 512)],
              writes=[f"vT{sl}"])

    def attn_chunk(ch):
        sl = ch % 2
        t0 = ch * C
        if ATTN_SUB < 1:
            return
        expw, Gx, Gi, tmpf = expw2[sl], Gx2[sl], Gi2[sl], tmpf2[sl]
        eGCs, rodU, kd, rd = eGC2[sl], rodU2[sl], kd2[sl], rd2[sl]
        kod, r0b, kCf, kCt = kod2[sl], r0b2[sl], kCf2[sl], kCt2[sl]
        tmpu, uS = tmpu2[sl], uS2[sl]
        K_ = {k: f"{k}{sl}" for k in ("expw", "Gx", "Gi", "tmpf", "eGCs", "rodU",
                                      "kd", "rd", "r0b", "kCf", "kCt", "tmpu", "uS",
                                      "kod1", "kod2", "kod3")}
        for b in range(2):
            def mmw(E, b=b):
                E.matmul(pb[4 + b][:, 0:512], tanhu[0:GLO, t0:t0 + C],
                         sWw2[0:GLO, 512 * b:512 * (b + 1)], start=True, stop=False)
                return E.matmul(pb[4 + b][:, 0:512], consts["onesf"][0:1, :],
                                sbw[0:1, 512 * b:512 * (b + 1)], start=False, stop=True)
            P.op("pe", mmw, reads=["tanhu", "sWw2", "sbw", "k_onesf"], writes=[PBK[4 + b]])
        for b in range(2):
            P.op("act", lambda E, b=b: E.activation(expw[:, 512 * b:512 * (b + 1)],
                                                    pb[4 + b][:, 0:512], AF.Exp),
                 reads=[PBK[4 + b]], writes=[K_["expw"]])
        for half, (gbuf, gkey, tri) in enumerate([(Gx, K_["Gx"], "ntri_excl"),
                                                  (Gi, K_["Gi"], "ntri_incl")]):
            for j in range(8):
                bank = 4 + 2 * half + (j // 4)
                def mmg(E, j=j, bank=bank, tri=tri):
                    return E.matmul(pb[bank][:, 128 * (j % 4):128 * (j % 4) + 128],
                                    expw[:, 128 * j:128 * (j + 1)], consts[tri][:, :],
                                    start=(j % 4 == 0), stop=(j % 4 == 3),
                                    skip_group_check=(j % 4 not in (0, 3)))
                P.op("pe", mmg, reads=[K_["expw"], "k_" + tri], writes=[PBK[bank]])
        for half, (gbuf, gkey) in enumerate([(Gx, K_["Gx"]), (Gi, K_["Gi"])]):
            for bb in range(2):
                bank = 4 + 2 * half + bb
                P.op("act", lambda E, bb=bb, bank=bank, gbuf=gbuf:
                     E.copy(gbuf[:, 512 * bb:512 * (bb + 1)], pb[bank][:, 0:512]),
                     reads=[PBK[bank]], writes=[gkey])
        if ATTN_SUB < 2:
            return
        gc_cols = Gi[:, 127:1024:128]
        P.op("act", lambda E: E.activation(eGCs[:, 0:8], gc_cols, AF.Exp),
             reads=[K_["Gi"]], writes=[K_["eGCs"]])
        P.op("dve", lambda E: E.tensor_tensor(tmpf[:, :], bcast(gc_cols, 128), Gi[:, :], ALU.subtract),
             reads=[K_["Gi"], K_["tmpf"]], writes=[K_["tmpf"]])
        P.op("act", lambda E: E.activation(kCf[:, :], tmpf[:, :], AF.Exp),
             reads=[K_["tmpf"]], writes=[K_["kCf"]])
        P.op("dve", lambda E, sl=sl: E.tensor_tensor(kCf[:, :], kCf[:, :], kF[sl][:, :], ALU.mult),
             reads=[K_["kCf"], f"kF{sl}"], writes=[K_["kCf"]])
        P.op("act", lambda E: E.activation(r0b[:, :], Gx[:, :], AF.Exp),
             reads=[K_["Gx"]], writes=[K_["r0b"]])
        P.op("dve", lambda E, sl=sl: E.tensor_tensor(r0b[:, :], r0b[:, :], rF[sl][:, :], ALU.mult),
             reads=[K_["r0b"], f"rF{sl}"], writes=[K_["r0b"]])
        midc = Gx[:, 16:1024:32]
        blkc = Gx[:, 0:1024:32]
        for kind, dstbuf, dstkey in [("kd", kd, K_["kd"]), ("rd", rd, K_["rd"]),
                                     ("rod", rodU, K_["rodU"])]:
            if kind == "kd":
                P.op("dve", lambda E: E.tensor_tensor(tmpf[:, :], bcast(midc, 32), Gi[:, :], ALU.subtract),
                     reads=[K_["Gx"], K_["Gi"], K_["tmpf"]], writes=[K_["tmpf"]])
            elif kind == "rd":
                P.op("dve", lambda E: E.tensor_tensor(tmpf[:, :], Gx[:, :], bcast(midc, 32), ALU.subtract),
                     reads=[K_["Gx"], K_["tmpf"]], writes=[K_["tmpf"]])
            else:
                P.op("dve", lambda E: E.tensor_tensor(tmpf[:, :], Gx[:, :], bcast(blkc, 32), ALU.subtract),
                     reads=[K_["Gx"], K_["tmpf"]], writes=[K_["tmpf"]])
            if kind in ("kd", "rd"):
                P.op("dve", lambda E: E.tensor_scalar_min(tmpf[:, :], tmpf[:, :], ACLAMP),
                     reads=[K_["tmpf"]], writes=[K_["tmpf"]])
            P.op("act", lambda E, dstbuf=dstbuf: E.activation(dstbuf[:, :], tmpf[:, :], AF.Exp),
                 reads=[K_["tmpf"]], writes=[dstkey])
            src, skey = (kF[sl], f"kF{sl}") if kind == "kd" else (rF[sl], f"rF{sl}")
            P.op("dve", lambda E, dstbuf=dstbuf, src=src:
                 E.tensor_tensor(dstbuf[:, :], dstbuf[:, :], src[:, :], ALU.mult),
                 reads=[dstkey, skey], writes=[dstkey])
        for I in (1, 2, 3):
            w_ = 32 * I
            aI = Gx[:, w_:1024:128]
            gi_sl = bass.AP(Gi.tensor, Gi.offset, [list(p) for p in Gi.ap[:1]] + [[128, 8], [1, w_]])
            kf_sl = bass.AP(kF[sl].tensor, kF[sl].offset, [list(p) for p in kF[sl].ap[:1]] + [[128, 8], [1, w_]])
            P.op("dve", lambda E, I=I, aI=aI, gi_sl=gi_sl, w_=w_:
                 E.tensor_tensor(kod[I][:, :].rearrange("p (a b) -> p a b", b=w_),
                                 bcast(aI, w_), gi_sl, ALU.subtract),
                 reads=[K_["Gx"], K_["Gi"], K_[f"kod{I}"]], writes=[K_[f"kod{I}"]])
            P.op("act", lambda E, I=I: E.activation(kod[I][:, :], kod[I][:, :], AF.Exp),
                 reads=[K_[f"kod{I}"]], writes=[K_[f"kod{I}"]])
            P.op("dve", lambda E, I=I, kf_sl=kf_sl, w_=w_:
                 E.tensor_tensor(kod[I][:, :].rearrange("p (a b) -> p a b", b=w_),
                                 kod[I][:, :].rearrange("p (a b) -> p a b", b=w_), kf_sl, ALU.mult),
                 reads=[K_[f"kod{I}"], f"kF{sl}"], writes=[K_[f"kod{I}"]])
        P.op("dve", lambda E, sl=sl: E.tensor_tensor(tmpu[:, :], rF[sl][:, :], kF[sl][:, :], ALU.mult),
             reads=[f"rF{sl}", f"kF{sl}", K_["tmpu"]], writes=[K_["tmpu"]])
        P.op("dve", lambda E: E.tensor_tensor(tmpu[:, :], tmpu[:, :],
                                              bcast(vt["bonus"][:, 0:8], 128), ALU.mult),
             reads=[K_["tmpu"], "vt_bonus"], writes=[K_["tmpu"]])
        if ATTN_SUB < 3:
            return
        def rec_tr(j):
            bk = 5 + (j % 2)
            bview = pb[bk][:, 0:64].bitcast(BF16)
            P.op("pe", lambda E, j=j, bview=bview: E.transpose(bview, kCf[:, 128 * j:128 * (j + 1)],
                                                               consts["ident"][:, :]),
                 reads=[K_["kCf"], "k_ident", PBK[bk]], writes=[PBK[bk]])
        def rec_trev(j):
            bk = 5 + (j % 2)
            bview = pb[bk][:, 0:64].bitcast(BF16)
            P.op("act", lambda E, j=j, bview=bview: E.copy(kCt[:, 128 * j:128 * (j + 1)], bview),
                 reads=[PBK[bk]], writes=[K_["kCt"]])
        rec_tr(0)
        for j in range(1, 8):
            rec_tr(j)
            rec_trev(j - 1)
        rec_trev(7)
        for j in range(8):
            P.op("pe", lambda E, j=j: E.matmul(pb[4][:, 4 * j:4 * j + 4], tmpu[:, 128 * j:128 * (j + 1)],
                                               consts["sel4"][:, :], start=(j == 0), stop=(j == 7)),
                 reads=[K_["tmpu"], "k_sel4", PBK[4]], writes=[PBK[4]])
        P.op("act", lambda E: E.copy(uS[:, :], pb[4][:, 0:32]), reads=[PBK[4]], writes=[K_["uS"]])
        if ATTN_SUB < 4:
            return
        for j in range(8):
            ob = j // 2
            P.op("pe", lambda E, j=j, ob=ob:
                 E.matmul(pb[ob][:, 256 * (j % 2):256 * (j % 2) + 256], r0b[:, 128 * j:128 * (j + 1)],
                          Sbf[:, 256 * j:256 * (j + 1)], start=(j % 2 == 0), stop=False,
                          skip_group_check=(j % 2 == 1)),
                 reads=[K_["r0b"], "Sbf", PBK[ob]], writes=[PBK[ob]])
        for j in range(8):
            sb_ = 4 + j // 2
            P.op("pe", lambda E, j=j, sb_=sb_, sl=sl:
                 E.matmul(pb[sb_][:, 256 * (j % 2):256 * (j % 2) + 256], kCt[:, 128 * j:128 * (j + 1)],
                          vT[sl][:, 256 * j:256 * (j + 1)], start=(j % 2 == 0), stop=(j % 2 == 1)),
                 reads=[K_["kCt"], f"vT{sl}", PBK[sb_]], writes=[PBK[sb_]])
        for j in range(8):
            sb_ = 4 + j // 2
            co = 256 * (j % 2)
            so = 256 * j
            P.op("dve", lambda E, sb_=sb_, co=co:
                 E.tensor_tensor(o2b[:, 0:256], pb[sb_][:, co:co + 256], consts["bdmask"][:, :], ALU.mult),
                 reads=[PBK[sb_], "k_bdmask", "o2b"], writes=["o2b"])
            P.op("dve", lambda E, j=j, so=so:
                 E.tensor_scalar(Sst[:, so:so + 256], Sst[:, so:so + 256],
                                 eGCs[:, j:j + 1], None, ALU.mult),
                 reads=["Sst", K_["eGCs"], "Sbf"], writes=["Sst"])
            P.op("dve", lambda E, so=so:
                 E.tensor_tensor(Sst[:, so:so + 256], Sst[:, so:so + 256], o2b[:, 0:256], ALU.add),
                 reads=["Sst", "o2b"], writes=["Sst"])
            P.op("act", lambda E, so=so:
                 E.copy(Sbf[:, so:so + 256], Sst[:, so:so + 256]),
                 reads=["Sst"], writes=["Sbf"])
        if ATTN_SUB < 5:
            return
        def rec_A(h):
            j, pr, hp = h // 4, 32 * (h % 4), h % 2
            R1, R2 = 4 + 2 * hp, 5 + 2 * hp
            tp = (pr, 0)
            def mmz(E, R1=R1):
                return E.matmul(pb[R1][:, 0:128], consts["zero32"][:, :], consts["zero32"][:, :],
                                start=True, stop=True)
            P.op("pe", mmz, reads=["k_zero32", PBK[R1]], writes=[PBK[R1]])
            def mmod(E, j=j, pr=pr, R1=R1, tp=tp):
                m = None
                for I in (1, 2, 3):
                    w_ = 32 * I
                    m = E.matmul(pb[R1][0:w_, 32 * I:32 * I + 32],
                                 kod[I][pr:pr + 32, w_ * j:w_ * j + w_],
                                 rodU[pr:pr + 32, 128 * j + w_:128 * j + w_ + 32],
                                 start=False, stop=True, skip_group_check=True, tile_position=tp)
                return m
            P.op("pe", mmod, reads=[K_["kod1"], K_["kod2"], K_["kod3"], K_["rodU"]], writes=[PBK[R1]])
            P.op("pe", lambda E, j=j, pr=pr, R2=R2, tp=tp:
                 E.matmul(pb[R2][:, 0:128], kd[pr:pr + 32, 128 * j:128 * (j + 1)],
                          rd[pr:pr + 32, 128 * j:128 * (j + 1)], start=True, stop=True,
                          tile_position=tp),
                 reads=[K_["kd"], K_["rd"], PBK[R2]], writes=[PBK[R2]])
        def rec_evac(h):
            hp, hq = h % 2, h % 4
            R1, R2 = 4 + 2 * hp, 5 + 2 * hp
            P.op("dve", lambda E, hq=hq, R2=R2:
                 E.tensor_tensor(At[hq][:, :], pb[R2][:, 0:128], consts["mask2"][:, :], ALU.mult),
                 reads=[PBK[R2], "k_mask2", f"At{hq}"], writes=[f"At{hq}"])
            P.op("dve", lambda E, hq=hq, R1=R1:
                 E.tensor_tensor(At[hq][:, :], At[hq][:, :], pb[R1][:, 0:128], ALU.add),
                 reads=[f"At{hq}", PBK[R1]], writes=[f"At{hq}"])
            P.op("act", lambda E, hp=hp, h=h:
                 E.activation(Du[hp][:, :], consts["ident"][:, :], AF.Copy, scale=uS[:, h:h + 1]),
                 reads=[K_["uS"], "k_ident", f"Du{hp}"], writes=[f"Du{hp}"])
        def rec_O(h):
            hq = h % 4
            ob, oc = h // 8, 64 * (h % 8)
            olast = (h % 8 == 7)
            P.op("pe", lambda E, hq=hq, ob=ob, oc=oc, h=h, sl=sl:
                 E.matmul(pb[ob][:, oc:oc + 64], At[hq][:, :], vT[sl][:, 64 * h:64 * h + 64],
                          start=False, stop=False, skip_group_check=True),
                 reads=[f"At{hq}", f"vT{sl}", PBK[ob]], writes=[PBK[ob]])
            hp = h % 2
            P.op("pe", lambda E, hp=hp, ob=ob, oc=oc, h=h, olast=olast, sl=sl:
                 E.matmul(pb[ob][:, oc:oc + 64], Du[hp][:, :], vT[sl][:, 64 * h:64 * h + 64],
                          start=False, stop=olast, skip_group_check=not olast),
                 reads=[f"Du{hp}", f"vT{sl}", PBK[ob]], writes=[PBK[ob]])
        rec_A(0)
        for h in range(1, NH):
            rec_A(h)
            rec_evac(h - 1)
            rec_O(h - 1)
        rec_evac(NH - 1)
        rec_O(NH - 1)
        if ATTN_SUB < 6:
            return
        for b in range(4):
            P.op("dve", lambda E, b=b:
                 E.tensor_reduce(st1[:, 8 * b:8 * b + 8],
                                 pb[b][:, 0:512].rearrange("p (h v) -> p h v", h=8),
                                 AX.X, ALU.add),
                 reads=[PBK[b], "st1"], writes=["st1"])
        for b in range(4):
            P.op("act", lambda E, b=b: E.activation(o2b[:, 512 * b:512 * (b + 1)], pb[b][:, 0:512],
                                                    AF.Square),
                 reads=[PBK[b], "o2b"], writes=["o2b"])
        for b in range(4):
            P.op("dve", lambda E, b=b:
                 E.tensor_reduce(st2[:, 8 * b:8 * b + 8],
                                 o2b[:, 512 * b:512 * (b + 1)].rearrange("p (h v) -> p h v", h=8),
                                 AX.X, ALU.add),
                 reads=["o2b", "st2"], writes=["st2"])
        P.op("dve", lambda E: E.tensor_scalar(stm[:, :], st1[:, :], 1.0 / HV, None, ALU.mult),
             reads=["st1", "stm"], writes=["stm"])
        P.op("dve", lambda E: E.tensor_tensor(stv[:, :], stm[:, :], stm[:, :], ALU.mult),
             reads=["stm", "stv"], writes=["stv"])
        P.op("dve", lambda E: E.scalar_tensor_tensor(stv[:, :], st2[:, :], 1.0 / HV, stv[:, :],
                                                     ALU.mult, ALU.subtract),
             reads=["st2", "stv"], writes=["stv"])
        P.op("act", lambda E: E.activation(stv[:, :], stv[:, :], AF.Ln, bias=csc[:, 0:1]),
             reads=["stv", "csc"], writes=["stv"])
        P.op("act", lambda E: E.activation(str_[:, :], stv[:, :], AF.Exp, scale=csc[:, 1:2]),
             reads=["stv", "csc"], writes=["str_"])
        for b in range(4):
            P.op("dve", lambda E, b=b:
                 E.tensor_tensor(o2b[:, 512 * b:512 * (b + 1)], pb[b][:, 0:512],
                                 bcast(stm[:, 8 * b:8 * b + 8], 64), ALU.subtract),
                 reads=[PBK[b], "stm", "o2b"], writes=["o2b"])
        for b in range(4):
            P.op("dve", lambda E, b=b:
                 E.tensor_tensor(onb[:, 512 * b:512 * (b + 1)], o2b[:, 512 * b:512 * (b + 1)],
                                 bcast(str_[:, 8 * b:8 * b + 8], 64), ALU.mult),
                 reads=["o2b", "str_", "onb"], writes=["onb"])
        def rec_ot(v16):
            bk = 5 + (v16 % 2)
            bview = pb[bk][:, 64:128].bitcast(BF16)
            P.op("pe", lambda E, v16=v16, bview=bview:
                 E.transpose(bview, onb[:, 128 * v16:128 * (v16 + 1)], consts["ident"][:, :]),
                 reads=["onb", "k_ident", PBK[bk]], writes=[PBK[bk]])
        def rec_ote(v16):
            bk = 5 + (v16 % 2)
            bview = pb[bk][:, 64:128].bitcast(BF16)
            P.op("act", lambda E, v16=v16, bview=bview:
                 E.copy(onT[:, 128 * v16:128 * (v16 + 1)], bview),
                 reads=[PBK[bk], ("onT", v16 // 4)], writes=[("onT", v16 // 4)])
        def rec_ogq(q):
            cq = slice(512 * q, 512 * (q + 1))
            P.op("dve", lambda E, q=q, cq=cq:
                 E.tensor_tensor(onT[:, cq], onT[:, cq],
                                 bcast(vt["gnw"][:, 4 * q:4 * q + 4], 128), ALU.mult),
                 reads=[("onT", q), "vt_gnw"], writes=[("onT", q)])
            P.op("dve", lambda E, q=q, cq=cq:
                 E.tensor_tensor(onT[:, cq], onT[:, cq],
                                 bcast(vt["gnb"][:, 4 * q:4 * q + 4], 128), ALU.add),
                 reads=[("onT", q), "vt_gnb"], writes=[("onT", q)])
            ogv = bass.AP(og.tensor, og.offset + 1024 * q + 128 * (ch % 2),
                          [list(og.ap[0]), [256, 4], [1, 128]])
            P.op("dve", lambda E, q=q, cq=cq, sl=sl, ogv=ogv:
                 E.tensor_tensor(ogv, onT[:, cq], sgF[sl][:, cq], ALU.mult),
                 reads=[("onT", q), f"sgF{sl}", ("og", q)], writes=[("og", q)])
        rec_ot(0)
        for v16 in range(1, 16):
            rec_ot(v16)
            rec_ote(v16 - 1)
            if v16 % 4 == 0:
                rec_ogq(v16 // 4 - 1)
        rec_ote(15)
        rec_ogq(3)

    def phase_c(chpair):
        t0 = 256 * chpair
        for tt in range(2):
            P.dma("sp", "xr" if tt == 0 else "xo", lambda E, tt=tt, t0=t0:
                  E.dma_start(sumb2[tt][:, :], xTok[t0 + 128 * tt:t0 + 128 * (tt + 1), :]),
                  reads=[f"sumb{tt}"], writes=[f"sumb{tt}"])
        for kv in range(16):
            wsl = kv % 2
            P.dma("sp", f"f{wsl}", lambda E, kv=kv, wsl=wsl:
                  E.dma_start(wob[wsl], wc["wo"][128 * kv:128 * (kv + 1), :]),
                  reads=["wcache", WOBK[wsl]], writes=[WOBK[wsl]])
            for tt in range(2):
                for q in range(4):
                    bank = 4 * tt + q
                    P.op("pe", lambda E, kv=kv, tt=tt, q=q, bank=bank, wsl=wsl:
                         E.matmul(pb[bank][:, 0:512], og[:, 256 * kv + 128 * tt:256 * kv + 128 * (tt + 1)],
                                  wob[wsl][:, 512 * q:512 * (q + 1)],
                                  start=(kv == 0), stop=(kv == 15)),
                         reads=[WOBK[wsl], ("og", kv // 4), PBK[bank]], writes=[PBK[bank]])
        for tt in range(2):
            for q in range(4):
                bank = 4 * tt + q
                P.op("dve", lambda E, tt=tt, q=q, bank=bank:
                     E.tensor_tensor(sumb2[tt][:, 512 * q:512 * (q + 1)], sumb2[tt][:, 512 * q:512 * (q + 1)],
                                     pb[bank][:, 0:512], ALU.add),
                     reads=[PBK[bank], f"sumb{tt}"], writes=[f"sumb{tt}"])
            P.op("dve", lambda E: E.memset(lnt[:, 0:16], 0.0), writes=["lnt"])
            P.op("dve", lambda E, tt=tt: E.tensor_reduce(lnt[:, 0:1], sumb2[tt][:, :], AX.X, ALU.add),
                 reads=[f"sumb{tt}", "lnt"], writes=["lnt"])
            for q in range(4):
                P.op("act", lambda E, q=q, tt=tt: E.activation(o2b[:, 512 * q:512 * (q + 1)],
                                                               sumb2[tt][:, 512 * q:512 * (q + 1)], AF.Square,
                                                               accum_out=lnt[:, 1 + q:2 + q]),
                     reads=[f"sumb{tt}", "lnt", "o2b"], writes=["o2b", "lnt"])
            P.op("dve", lambda E: E.tensor_reduce(lnt[:, 5:6], lnt[:, 1:5], AX.X, ALU.add),
                 reads=["lnt"], writes=["lnt"])
            P.op("dve", lambda E: E.tensor_scalar(lnt[:, 6:7], lnt[:, 0:1], 1.0 / D, None, ALU.mult),
                 reads=["lnt"], writes=["lnt"])
            P.op("dve", lambda E: E.tensor_tensor(lnt[:, 7:8], lnt[:, 6:7], lnt[:, 6:7], ALU.mult),
                 reads=["lnt"], writes=["lnt"])
            P.op("dve", lambda E: E.scalar_tensor_tensor(lnt[:, 8:9], lnt[:, 5:6], 1.0 / D,
                                                         lnt[:, 7:8], ALU.mult, ALU.subtract),
                 reads=["lnt"], writes=["lnt"])
            P.op("act", lambda E: E.activation(lnt[:, 9:10], lnt[:, 8:9], AF.Ln, bias=csc[:, 0:1]),
                 reads=["lnt", "csc"], writes=["lnt"])
            P.op("act", lambda E: E.activation(lnt[:, 10:11], lnt[:, 9:10], AF.Exp, scale=csc[:, 1:2]),
                 reads=["lnt", "csc"], writes=["lnt"])
            P.op("dve", lambda E: E.tensor_tensor(lnt[:, 11:12], lnt[:, 6:7], lnt[:, 10:11], ALU.mult),
                 reads=["lnt"], writes=["lnt"])
            P.op("dve", lambda E: E.tensor_scalar(lnt[:, 12:13], lnt[:, 11:12], -1.0, None, ALU.mult),
                 reads=["lnt"], writes=["lnt"])
            P.op("dve", lambda E, tt=tt: E.tensor_scalar(x1v[tt], sumb2[tt][:, :],
                                                         lnt[:, 10:11], lnt[:, 12:13],
                                                         ALU.mult, ALU.add),
                 reads=[f"sumb{tt}", "lnt", X1K[tt]], writes=[X1K[tt]])
            P.dma("act", "x1q", lambda E, tt=tt, t0=t0:
                  E.dma_start(x1_s[t0 + 128 * tt:t0 + 128 * (tt + 1), :], x1v[tt]),
                  reads=[X1K[tt]], writes=[("x1w", t0, tt)])

    NCHx = min(NCH, ATTN_NCH) if ATTN_NCH else NCH
    load_chunk_streams(0)
    for ch in range(NCHx):
        if ch + 1 < NCHx:
            load_chunk_streams(ch + 1)
        attn_chunk(ch)
        if ch % 2 == 1 and ATTN_SUB >= 7:
            phase_c(ch // 2)

    if stage == "attn":
        P.barrier()
        for s_ in P.dma_streams:
            if P.dma_cnt[s_]:
                P.barrier_wait("sp", s_, P.dma_total(s_))
        P.op("dve", lambda E: E.memset(sumb2[0][0:1, 0:4], 0.0), writes=["sumb0"])
        P.dma("sp", "outq", lambda E: E.dma_start(out[0:1, 0:4], sumb2[0][0:1, 0:4]), reads=["sumb0"])
        P.barrier_wait("sp", "outq", P.dma_total("outq"))
        P.build()
        stack.close()
        return nc, dbg_out

    P.barrier()
    AR.reset()
    x1T = AR.bf(16 * FB)
    gu = AR.bf(44 * FB)
    gwb = [AR.bf(2048) for _ in range(3)]
    uwb = [AR.bf(2048) for _ in range(3)]
    fdw = [AR.bf(2048) for _ in range(3)]
    gtmp = [AR.bf(512) for _ in range(3)]
    x1k = [AR.bf(2048) for _ in range(2)]
    sumf = AR.f32(2048)
    sqf = AR.f32(2048)
    _ob = AR.f32(2048)
    outb = [_ob, _ob]
    lnf = AR.f32(32)
    NMH = FH // 128

    def ffn_w_load(cache, src_int, mh, dstbuf, dstkey, ldst, wrst, first):
        if first:
            sl = dstkey
            isl = mh % 2
            P.dma("sp", ldst, lambda E, mh=mh, isl=isl:
                  E.dma_start(ldf[isl][:, :].bitcast(I32), src_int[mh, :, :]),
                  writes=[f"ldf{isl}"])
            P.op("act", lambda E, isl=isl, dstbuf=dstbuf:
                 E.activation(dstbuf[:, :], ldf[isl][:, :].bitcast(I32), AF.Copy, bias=-128.0),
                 reads=[f"ldf{isl}"], writes=[dstkey])
            P.dma("act", wrst,
                  lambda E, mh=mh, dstbuf=dstbuf: E.dma_start(cache[mh, :, :], dstbuf[:, :]),
                  reads=[dstkey], writes=[("fcache", id(cache), mh)])
        else:
            P.dma("sp", ldst, lambda E, mh=mh, dstbuf=dstbuf:
                  E.dma_start(dstbuf[:, :], cache[mh, :, :]),
                  reads=[("fcache", id(cache), mh)], writes=[dstkey])

    for fb in range(NFB):
        t0f = fb * FB
        x1keys = [("x1w", 256 * p, tt) for p in range(t0f // 256, (t0f + FB) // 256)
                  for tt in range(2)]
        for dt in range(16):
            strm = "f0" if dt % 2 == 0 else "f1"
            P.dma("sp", strm, lambda E, dt=dt, t0f=t0f:
                  E.dma_start(x1T[:, FB * dt:FB * (dt + 1)],
                              x1_s[t0f:t0f + FB, 128 * dt:128 * (dt + 1)], transpose=True),
                  reads=x1keys, writes=["x1T"])
        for mh in range(NMH):
            gsl = mh % 3
            first = (fb == 0)
            ffn_w_load(fg_s, wq_gT, mh, gwb[gsl], f"gwb{gsl}", f"fg{gsl}", "fw0", first)
            ffn_w_load(fu_s, wq_uT, mh, uwb[gsl], f"uwb{gsl}", f"fu{gsl}", "fw1", first)
            gb, ub = 2 * (mh % 2), 2 * (mh % 2) + 1
            def mmg(E, mh=mh, gsl=gsl, gb=gb):
                for kt in range(16):
                    m = E.matmul(pb[gb][:, 0:FB], gwb[gsl][:, 128 * kt:128 * (kt + 1)],
                                 x1T[:, FB * kt:FB * kt + FB], start=(kt == 0), stop=(kt == 15))
                return m
            P.op("pe", mmg, reads=[f"gwb{gsl}", "x1T"], writes=[PBK[gb]])
            def mmu(E, mh=mh, gsl=gsl, ub=ub):
                for kt in range(16):
                    m = E.matmul(pb[ub][:, 0:FB], uwb[gsl][:, 128 * kt:128 * (kt + 1)],
                                 x1T[:, FB * kt:FB * kt + FB], start=(kt == 0), stop=(kt == 15))
                return m
            P.op("pe", mmu, reads=[f"uwb{gsl}", "x1T"], writes=[PBK[ub]])
            P.op("act", lambda E, mh=mh, gsl=gsl, gb=gb:
                 E.activation(gtmp[gsl][:, 0:FB], pb[gb][:, 0:FB], AF.Silu,
                              scale=vt["sg"][:, mh:mh + 1]),
                 reads=[PBK[gb], "vt_sg"], writes=[f"gtmp{gsl}"])
            P.op("dve", lambda E, mh=mh, gsl=gsl, ub=ub:
                 E.scalar_tensor_tensor(gu[:, FB * mh:FB * (mh + 1)], pb[ub][:, 0:FB],
                                        vt["su"][:, mh:mh + 1], gtmp[gsl][:, 0:FB],
                                        ALU.mult, ALU.mult),
                 reads=[PBK[ub], "vt_su", f"gtmp{gsl}"], writes=["gu"])
        for grp in range(FB // 256):
            for mh in range(NMH):
                dsl = mh % 3
                ffn_w_load(fd_s, wq_dT, mh, fdw[dsl], f"fdw{dsl}", f"fd{dsl}", "fw2",
                           (fb == 0 and grp == 0))
                def mmd(E, mh=mh, grp=grp, dsl=dsl):
                    m = None
                    for tt2 in range(2):
                        for q in range(4):
                            m = E.matmul(pb[4 * tt2 + q][:, 0:512],
                                         gu[:, FB * mh + 256 * grp + 128 * tt2:FB * mh + 256 * grp + 128 * (tt2 + 1)],
                                         fdw[dsl][:, 512 * q:512 * (q + 1)],
                                         start=(mh == 0), stop=(mh == NMH - 1),
                                         skip_group_check=(mh not in (0, NMH - 1)))
                    return m
                P.op("pe", mmd, reads=[f"fdw{dsl}", "gu"], writes=[PBK[0], PBK[1], PBK[2], PBK[3],
                                                                   PBK[4], PBK[5], PBK[6], PBK[7]])
            for tt2 in range(2):
                tg = t0f + 256 * grp + 128 * tt2
                ksl = tt2 % 2
                P.dma("sp", "xr" if ksl == 0 else "xo", lambda E, tg=tg, ksl=ksl:
                      E.dma_start(x1k[ksl][:, :], x1_s[tg:tg + 128, :]),
                      reads=x1keys, writes=[f"x1k{ksl}"])

                for q in range(4):
                    bank = 4 * tt2 + q
                    P.op("dve", lambda E, q=q, bank=bank:
                         E.tensor_tensor(sumf[:, 512 * q:512 * (q + 1)], pb[bank][:, 0:512],
                                         sdb[:, 512 * q:512 * (q + 1)], ALU.mult),
                         reads=[PBK[bank], "sdb"], writes=["sumf"])
                for q in range(4):
                    P.op("dve", lambda E, q=q, ksl=ksl:
                         E.tensor_tensor(sumf[:, 512 * q:512 * (q + 1)], sumf[:, 512 * q:512 * (q + 1)],
                                         x1k[ksl][:, 512 * q:512 * (q + 1)], ALU.add),
                         reads=["sumf", f"x1k{ksl}"], writes=["sumf"])
                P.op("dve", lambda E: E.memset(lnf[:, 0:16], 0.0), writes=["lnf"])
                P.op("dve", lambda E: E.tensor_reduce(lnf[:, 0:1], sumf[:, :], AX.X, ALU.add),
                     reads=["sumf", "lnf"], writes=["lnf"])
                for q in range(4):
                    P.op("act", lambda E, q=q: E.activation(sqf[:, 512 * q:512 * (q + 1)],
                                                            sumf[:, 512 * q:512 * (q + 1)], AF.Square,
                                                            accum_out=lnf[:, 1 + q:2 + q]),
                         reads=["sumf", "lnf"], writes=["sqf", "lnf"])
                P.op("dve", lambda E: E.tensor_reduce(lnf[:, 5:6], lnf[:, 1:5], AX.X, ALU.add),
                     reads=["lnf"], writes=["lnf"])
                P.op("dve", lambda E: E.tensor_scalar(lnf[:, 6:7], lnf[:, 0:1], 1.0 / D, None, ALU.mult),
                     reads=["lnf"], writes=["lnf"])
                P.op("dve", lambda E: E.tensor_tensor(lnf[:, 7:8], lnf[:, 6:7], lnf[:, 6:7], ALU.mult),
                     reads=["lnf"], writes=["lnf"])
                P.op("dve", lambda E: E.scalar_tensor_tensor(lnf[:, 8:9], lnf[:, 5:6], 1.0 / D,
                                                             lnf[:, 7:8], ALU.mult, ALU.subtract),
                     reads=["lnf"], writes=["lnf"])
                P.op("act", lambda E: E.activation(lnf[:, 9:10], lnf[:, 8:9], AF.Ln, bias=csc[:, 0:1]),
                     reads=["lnf", "csc"], writes=["lnf"])
                P.op("act", lambda E: E.activation(lnf[:, 10:11], lnf[:, 9:10], AF.Exp, scale=csc[:, 1:2]),
                     reads=["lnf", "csc"], writes=["lnf"])
                P.op("dve", lambda E: E.tensor_tensor(lnf[:, 11:12], lnf[:, 6:7], lnf[:, 10:11], ALU.mult),
                     reads=["lnf"], writes=["lnf"])
                P.op("dve", lambda E: E.tensor_scalar(lnf[:, 12:13], lnf[:, 11:12], -1.0, None, ALU.mult),
                     reads=["lnf"], writes=["lnf"])
                P.op("dve", lambda E, tt2=tt2: E.tensor_scalar(outb[tt2][:, :], sumf[:, :],
                                                               lnf[:, 10:11], lnf[:, 12:13],
                                                               ALU.mult, ALU.add),
                     reads=["sumf", "lnf"], writes=["outb0"])
                P.dma("act", "outq", lambda E, tt2=tt2, tg=tg:
                      E.dma_start(out[tg:tg + 128, :], outb[tt2][:, :]),
                      reads=["outb0"], writes=[("outw", tg)])

    P.barrier_wait("sp", "outq", P.dma_total("outq"))
    P.build()
    stack.close()
    return nc, dbg_out


def make_inputs_per_core(b, ins, T=2048):
    x = np.asarray(ins["L0_attn_hidden_states"][b])
    m = {}
    m["x"] = np.ascontiguousarray(x)
    m["xT"] = np.ascontiguousarray(x.T)
    m["Wx1"] = np.asarray(ins["L0_attn_Wx1"])
    m["Wx2"] = np.ascontiguousarray(np.asarray(ins["L0_attn_Wx2"]).reshape(5 * RLO, D))
    m["Wr"] = np.asarray(ins["L0_attn_Wr"])
    m["Ww1"] = np.asarray(ins["L0_attn_Ww1"])
    m["Ww2"] = np.asarray(ins["L0_attn_Ww2"])
    m["bw"] = np.asarray(ins["L0_attn_bw"]).reshape(1, DK)
    m["Wk"] = np.asarray(ins["L0_attn_Wk"])
    m["Wv"] = np.asarray(ins["L0_attn_Wv"])
    m["Wg"] = np.asarray(ins["L0_attn_Wg"])
    m["Wo"] = np.asarray(ins["L0_attn_Wo"])
    def col_img(w):
        wt = np.asarray(w).T
        return np.ascontiguousarray(
            wt.reshape(16, 128, FH // 128, 128).transpose(2, 1, 0, 3).reshape(FH // 128, 128, D))
    m["wq_gT"] = col_img(ins["L0_ffn_wq_gate"])
    m["wq_uT"] = col_img(ins["L0_ffn_wq_up"])
    m["wq_dT"] = np.ascontiguousarray(
        np.asarray(ins["L0_ffn_wq_down"]).T.reshape(FH // 128, 128, D))
    m["s_d"] = np.asarray(ins["L0_ffn_s_down"]).reshape(1, D)

    def tilecol(v, nt):
        return np.ascontiguousarray(np.asarray(v).reshape(nt, 128).T)
    m["vt_mu"] = tilecol(ins["L0_attn_mu_x"], 16)
    m["vt_bonus"] = tilecol(np.asarray(ins["L0_attn_bonus"]).reshape(NH * HK), 8)
    m["vt_gnw"] = tilecol(np.tile(np.asarray(ins["L0_attn_gn_w"]), NH), 16)
    m["vt_gnb"] = tilecol(np.tile(np.asarray(ins["L0_attn_gn_b"]), NH), 16)
    m["vt_sg"] = tilecol(ins["L0_ffn_s_gate"], 44)
    m["vt_su"] = tilecol(ins["L0_ffn_s_up"], 44)
    for k, v in host_constants().items():
        m["c_" + k] = v
    return m


def kernel(**inputs):
    T = 2048
    nc, _ = build_program(T=T, stage="full")
    in_maps = [make_inputs_per_core(b, inputs, T) for b in range(BATCH)]
    res = run_bass_kernel_spmd(nc, in_maps, core_ids=list(range(BATCH)))
    return np.stack([r["out"] for r in res.results], axis=0)
